# version 16
# baseline (speedup 1.0000x reference)
import contextlib
import numpy as np
import concourse.bass as bass
import concourse.mybir as mybir
from concourse.bass_utils import run_bass_kernel_spmd

F32 = mybir.dt.float32
BF16 = mybir.dt.bfloat16
U8 = mybir.dt.uint8
AF = mybir.ActivationFunctionType
ALU = mybir.AluOpType
AX = mybir.AxisListType.X

NCORES = 8
T = 16384
C = 2048
TC = T // NCORES
NT = TC // 128
CT = C // 128
H = 16
FF = 8192
NFB = FF // 128
DHALF = 1024

KSLOT = 129
KV_COLS = H * KSLOT
VSLOT = 144

A_OFF = 0
B_OFF = 65536
S_OFF = 131072
Q_OFF = 163840
T_OFF = 196608
C_OFF = 209600
ARENA_BYTES = 212800


class Prog:
    def __init__(self, which, handle, sems):
        self.which = which
        self.handle = handle
        self.sems = sems
        self.c = {k: 0 for k in sems}

    def do(self, eng, fn):
        if self.which == eng:
            return fn(self.handle)

    CHAIN = {"ve": "vm", "sc": "sm"}

    def inc(self, eng, fn, sem, n=1):
        ch = self.CHAIN.get(eng)
        if ch is not None and self.which == eng:
            self.handle.wait_ge(self.sems[ch], self.c[ch])
        self.c[sem] += n
        if self.which == eng:
            fn(self.handle).then_inc(self.sems[sem], n)
        return self.c[sem]

    def dma(self, eng, fn, stream):
        tot = self.c[stream]
        if self.which == eng and tot > 0:
            self.handle.wait_ge(self.sems[stream], tot)
        self.c[stream] += 16
        if self.which == eng:
            fn(self.handle).then_inc(self.sems[stream], 16)
        return (stream, self.c[stream])

    def waitv(self, eng, sv):
        self.wait(eng, sv[0], sv[1])

    def inc_default(self, eng, fn, sem):
        self.c[sem] += 1
        if self.which == eng:
            fn(self.handle).then_inc(self.sems[sem])
        return self.c[sem]

    def wait(self, eng, sem, val):
        assert val <= self.c[sem], (eng, sem, val, self.c[sem])
        if self.which == eng:
            self.handle.wait_ge(self.sems[sem], val)


def build_nc():
    nc = bass.Bass("TRN2", target_bir_lowering=False, debug=False)

    x_in = nc.dram_tensor("x", [TC, C], F32, kind="ExternalInput").ap()
    xT_in = nc.dram_tensor("xT", [C, TC], F32, kind="ExternalInput").ap()
    wqT = nc.dram_tensor("wqT", [C, C], F32, kind="ExternalInput").ap()
    wkT = nc.dram_tensor("wkT", [C, C], F32, kind="ExternalInput").ap()
    wvT = nc.dram_tensor("wvT", [C, C], F32, kind="ExternalInput").ap()
    woT = nc.dram_tensor("woT", [C, C], F32, kind="ExternalInput").ap()
    w1T = nc.dram_tensor("w1T", [C, FF], F32, kind="ExternalInput").ap()
    w2T = nc.dram_tensor("w2T", [FF, C], F32, kind="ExternalInput").ap()
    b1c_in = nc.dram_tensor("b1c", [128, NFB], F32, kind="ExternalInput").ap()
    b2r_in = nc.dram_tensor("b2r", [1, C], F32, kind="ExternalInput").ap()
    out_t = nc.dram_tensor("out", [TC, C], F32, kind="ExternalOutput").ap()

    x2s = nc.dram_tensor("x2s", [TC, C], BF16).ap()
    hts = nc.dram_tensor("hts", [FF, TC], BF16).ap()
    w2qs = nc.dram_tensor("w2qs", [FF, C], BF16).ap()
    pre0s = nc.dram_tensor("pre0s", [TC, DHALF], F32).ap()
    cc_in1 = nc.dram_tensor("cc_in1", [128, KV_COLS // 2], F32)
    cc_out1 = nc.dram_tensor("cc_out1", [128, KV_COLS // 2], F32, addr_space="Shared")
    cc_in2 = nc.dram_tensor("cc_in2", [128, KV_COLS // 2], F32)
    cc_out2 = nc.dram_tensor("cc_out2", [128, KV_COLS // 2], F32, addr_space="Shared")

    with contextlib.ExitStack() as es:
        E = es.enter_context
        arena = E(nc.sbuf_tensor("arena", [128, ARENA_BYTES], U8))
        ar = arena.ap()

        def V(off, cols, dt, rows=128, tag=""):
            nbytes = cols * (4 if dt == F32 else 2)
            assert off % 32 == 0 and off + nbytes <= ARENA_BYTES, (tag, off, nbytes)
            return ar[0:rows, off:off + nbytes].bitcast(dt)

        co = [C_OFF]
        def CV(cols, dt, rows=128):
            v = V(co[0], cols, dt, rows)
            co[0] += ((cols * (4 if dt == F32 else 2) + 63) // 64) * 64
            return v
        ones_row_f = CV(128, F32, rows=1)
        ones_row_b = CV(128, BF16, rows=1)
        ones_col_f = CV(1, F32)
        eps_ln = CV(1, F32)
        thr1p, thr1n = CV(1, F32), CV(1, F32)
        thr2p, thr2n = CV(1, F32), CV(1, F32)
        c_b1 = CV(NFB, F32)
        wsum = CV(64, F32)
        c_red = CV(1, F32)
        thr_tmp = CV(4, F32, rows=1)
        eps_z6 = CV(1, F32, rows=1)
        ln_s1 = [CV(1, F32), CV(1, F32)]
        ln_s2 = [CV(1, F32), CV(1, F32)]
        ln_mean = [CV(1, F32), CV(1, F32)]
        ln_var = [CV(1, F32), CV(1, F32)]
        ln_sd = [CV(1, F32), CV(1, F32)]
        ln_rstd = [CV(1, F32), CV(1, F32)]
        ln_nm = [CV(1, F32), CV(1, F32)]
        s1a = CV(NT, F32)
        s2a = CV(NT, F32)
        assert co[0] <= ARENA_BYTES, co

        A_xT = V(A_OFF, CT * TC, BF16, tag="A")
        A_woT = A_xT
        A_x2T = A_xT
        A_w2hi = A_xT
        B_wkT = V(B_OFF, CT * C, BF16, tag="B")
        B_yT = B_wkT
        B_w1q = [V(B_OFF + i * 16384, CT * 512, BF16, tag="Bw1") for i in range(4)]
        B_w2lo = B_wkT
        S_wv = [V(S_OFF + i * 16384, CT * 512, BF16, tag="Swv") for i in range(2)]
        S_wq = S_wv
        S_xres = [V(S_OFF + i * 8192, C, F32, tag="Sxr") for i in range(2)]
        S_x2bf = [V(S_OFF + 16384 + i * 4096, C, BF16, tag="Sx2") for i in range(2)]
        S_hstage = [V(S_OFF + i * 4096, C, BF16, tag="Shs") for i in range(2)]
        S_ht2 = [V(S_OFF + i * 16384, 32 * 256, BF16, tag="Sht2") for i in range(2)]
        T_x2res = [V(T_OFF + i * 2048, DHALF, BF16, tag="Tx2r") for i in range(2)]
        T_ost = [V(T_OFF + 4096 + i * 2048, 512, F32, tag="Tost") for i in range(2)]
        Q_p1stg = [V(Q_OFF + i * 4096, C, BF16, tag="Qp1") for i in range(2)]
        Q_absd = V(Q_OFF + 8192, C, BF16, tag="Qab")
        Q_fmin = [V(Q_OFF + 12288 + i * 2048, 512, F32, tag="Qfm") for i in range(2)]
        Q_fexp = [V(Q_OFF + 16384 + i * 2048, 512, F32, tag="Qfe") for i in range(2)]
        Q_qT = [V(Q_OFF + 20480 + i * 4096, TC, BF16, tag="QqT") for i in range(2)]
        Q_zb = [V(Q_OFF + 28672 + i * 2048, 512, F32, tag="Qzb") for i in range(2)]
        kv_loc = V(Q_OFF + 20480, KV_COLS, F32, tag="kvloc")
        Q_pre = [V(Q_OFF + 16384 + i * 8192, C, F32, tag="Qpre") for i in range(2)]
        Q_q2stg = [V(Q_OFF + i * 2048, 512, F32, tag="Qq2") for i in range(4)]
        Q_sgnA = V(Q_OFF + 8192, 1024, BF16, tag="QsA")
        Q_sgnB = V(Q_OFF + 10240, 1024, BF16, tag="QsB")
        Q_w2stg = [V(Q_OFF + 12288 + i * 4096, 1024, F32, tag="Qw2s") for i in range(2)]
        Q_sgn2A = V(Q_OFF + 20480, 1024, BF16, tag="Qs2A")
        Q_sgn2B = V(Q_OFF + 22528, 1024, BF16, tag="Qs2B")
        Q_w2out = [V(Q_OFF + 24576 + i * 2048, 1024, BF16, tag="Qw2o") for i in range(2)]
        Q_vg = [V(Q_OFF + 28672 + i * 2048, 1024, BF16, tag="Qvg") for i in range(2)]
        Q_m2pre = [V(Q_OFF + i * 4096, DHALF, F32, tag="Qm2p") for i in range(2)]
        Q_r2 = [V(Q_OFF + 8192 + i * 4096, DHALF, F32, tag="Qr2") for i in range(2)]
        Q_b2b = V(Q_OFF + 16384, DHALF, F32, tag="Qb2b")
        Q_p0rd = [V(Q_OFF + 20480 + i * 4096, DHALF, F32, tag="Qp0") for i in range(2)]
        Q_b2row = V(Q_OFF + 28672, DHALF, F32, rows=1, tag="Qb2r")
        T_kbf = [V(T_OFF + i * 1024, 512, BF16, tag="Tk") for i in range(2)]
        T_vaug = [V(T_OFF + 2048 + i * 1280, 4 * VSLOT, BF16, tag="Tv") for i in range(2)]
        T_kvbf = V(T_OFF + 4608, KV_COLS, BF16, tag="Tkv")
        T_sq = V(T_OFF + 8832, C, BF16, tag="Tsq")
        z_ln = V(T_OFF + 8832, 512, F32, rows=1, tag="zln")
        z_bf = [V(T_OFF + i * 1024, 512, BF16, rows=1, tag="zbf")
                for i in range(4)]
        assert T_OFF + 8832 + 2 * C <= C_OFF

        psA = E(nc.psum_tensor("psA", [128, 2048], F32))
        psB = E(nc.psum_tensor("psB", [128, 2048], F32))

        sems = {}
        stream_names = (
            ["tm", "vm", "sm", "cc", "mi"]
            + [f"xt{i}" for i in range(4)] + [f"wk{i}" for i in range(4)]
            + ["wv0", "wv1", "q1a", "q1b", "wo0", "wo1", "mig",
               "x2T0", "x2T1", "x2T2", "x2T3",
               "xr0", "xr1", "x20", "x21", "p2a", "p2b", "p3a", "p3b",
               "p4a", "p4b",
               "wqs0", "wqs1", "ht0", "ht1", "wh0", "wh1",
               "hs0", "hs1", "hs2", "x2r0", "x2r1",
               "p00", "p01", "p0r0", "p0r1", "so0", "so1"])
        for name in stream_names:
            sems[name] = E(nc.semaphore(name))
        block = E(nc.Block())

        def prog(P: Prog):
            P.inc("ve", lambda e: e.memset(ones_row_f, 1.0), "vm")
            P.inc("ve", lambda e: e.memset(ones_row_b, 1.0), "vm")
            P.inc("ve", lambda e: e.memset(ones_col_f, 1.0), "vm")
            P.inc("ve", lambda e: e.memset(eps_ln, 1e-5), "vm")
            P.inc("ve", lambda e: e.memset(eps_z6, 1e-6), "vm")
            P.inc("ve", lambda e: e.memset(T_vaug[0][:, :], 0.0), "vm")
            P.inc("ve", lambda e: e.memset(T_vaug[1][:, :], 0.0), "vm")
            for par in range(2):
                for hh in range(4):
                    P.inc("ve", lambda e, par=par, hh=hh: e.memset(
                        T_vaug[par][:, hh * VSLOT + 128: hh * VSLOT + 129], 1.0), "vm")
            vm_setup_done = P.c["vm"]

            sv_b1 = P.dma("sy", lambda e: e.dma_start(c_b1, b1c_in[:, :]), "mi")

            sv_xt = [None] * CT
            sv_wk = [None] * CT
            def _wslot_load(e, src_t, g, par):
                dst = S_wv[par].rearrange("p (ct o) -> p ct o", ct=CT)
                src = src_t[:, g * 512:(g + 1) * 512].rearrange(
                    "(ct p) o -> p ct o", p=128)
                return e.dma_start(dst, src)
            sv_wv0 = P.dma("gp", lambda e: _wslot_load(e, wvT, 0, 0), "wv0")
            for ct in range(CT):
                sv_xt[ct] = P.dma("gp", lambda e, ct=ct: e.dma_start(
                    A_xT[:, ct * TC:(ct + 1) * TC],
                    xT_in[ct * 128:(ct + 1) * 128, :]), f"xt{ct % 4}")
                sv_wk[ct] = P.dma("gp", lambda e, ct=ct: e.dma_start(
                    B_wkT[:, ct * C:(ct + 1) * C],
                    wkT[ct * 128:(ct + 1) * 128, :]), f"wk{ct % 4}")
            sv_wv = [None] * 4
            sv_wv[0] = sv_wv0
            sv_wv[1] = P.dma("gp", lambda e: _wslot_load(e, wvT, 1, 1), "wv1")

            vm_kv_evac = [0] * 4
            sm_vaug = {}
            tm_kmm = {}
            tm_kvmm = {}
            vm_fm = {}
            sm_p1chunk = {}

            def p1_pass1_chunk(i, src_t):
                rt, fc = divmod(i, 4)
                par = i % 2
                if i >= 2:
                    P.wait("gp", *sm_p1chunk[i - 2])
                sv_here = P.dma("gp", lambda e: e.dma_start(
                    Q_p1stg[par],
                    src_t[rt * 128:(rt + 1) * 128, fc * 2048:(fc + 1) * 2048]),
                    "q1a" if par == 0 else "q1b")
                if par == 0:
                    P.waitv("sc", sv_here)
                    v = P.inc("sc", lambda e, i=i: e.activation(
                        Q_absd, Q_p1stg[0], AF.Abs, accum_out=wsum[:, i:i + 1]), "sm")
                    sm_p1chunk[i] = ("sm", v)
                else:
                    P.waitv("ve", sv_here)
                    v = P.inc("ve", lambda e, i=i: e.tensor_reduce(
                        wsum[:, i:i + 1], Q_p1stg[1], AX, ALU.add,
                        apply_absolute_value=True), "vm")
                    sm_p1chunk[i] = ("vm", v)

            sv_wq0_early = [None]

            def _emit_kv_acc(g, tt):
                kpar = tt % 2
                P.wait("te", "vm", vm_fm[(g, tt)])
                P.wait("te", "sm", sm_vaug[(g, tt)])
                if tt == 0 and g > 0:
                    P.wait("te", "vm", vm_kv_evac[g - 1])
                def _kv_acc(t, tt=tt, kpar=kpar):
                    for hh in range(4):
                        mm = t.matmul(
                            psB[:, hh * 512: hh * 512 + 132],
                            T_kbf[kpar][:, hh * 128:(hh + 1) * 128],
                            T_vaug[kpar][:, hh * VSLOT: hh * VSLOT + 132],
                            start=(tt == 0), stop=(tt == NT - 1))
                    return mm
                tm_kvmm[(g, tt)] = P.inc("te", _kv_acc, "tm")

            for g in range(4):
                for tt in range(NT):
                    kpar = tt % 2
                    k_sl = psA[:, kpar * 512:(kpar + 1) * 512]
                    v_sl = psA[:, 1024 + kpar * 512: 1024 + (kpar + 1) * 512]
                    if g > 0 or tt > 0:
                        P.waitv("te", sv_wv[g])
                    prev = (g, tt - 2) if tt >= 2 else (g - 1, NT - 2 + tt)
                    if prev in vm_fm:
                        P.wait("te", "vm", vm_fm[prev])
                        P.wait("te", "sm", sm_vaug[prev])
                    if g == 0 and tt == 0:
                        P.wait("te", "vm", vm_setup_done)
                        P.waitv("te", sv_wv[0])
                        for ct in range(CT):
                            P.waitv("te", sv_xt[ct])
                            P.waitv("te", sv_wk[ct])
                            def _kv_ct(t, ct=ct):
                                lhs = A_xT[:, ct * TC: ct * TC + 128]
                                t.matmul(k_sl, lhs,
                                         B_wkT[:, ct * C: ct * C + 512],
                                         start=(ct == 0), stop=(ct == CT - 1))
                                return t.matmul(v_sl, lhs,
                                         S_wv[0][:, ct * 512:(ct + 1) * 512],
                                         start=(ct == 0), stop=(ct == CT - 1))
                            tm_here = P.inc("te", _kv_ct, "tm")
                    else:
                        def _kv_proj(t, g=g, tt=tt, k_sl=k_sl, v_sl=v_sl):
                            for ct in range(CT):
                                lhs = A_xT[:, ct * TC + tt * 128:
                                           ct * TC + tt * 128 + 128]
                                t.matmul(k_sl, lhs,
                                         B_wkT[:, ct * C + g * 512:
                                               ct * C + g * 512 + 512],
                                         start=(ct == 0), stop=(ct == CT - 1))
                                mm = t.matmul(v_sl, lhs,
                                         S_wv[g % 2][:, ct * 512:(ct + 1) * 512],
                                         start=(ct == 0), stop=(ct == CT - 1))
                            return mm
                        tm_here = P.inc("te", _kv_proj, "tm")
                    tm_kmm[(g, tt)] = tm_here

                    if tt > 0:
                        _emit_kv_acc(g, tt - 1)

                    P.wait("ve", "tm", tm_here)
                    if prev in tm_kvmm:
                        P.wait("ve", "tm", tm_kvmm[prev])
                    vmin = P.inc("ve", lambda e, k_sl=k_sl, kpar=kpar: e.tensor_scalar(
                        Q_fmin[kpar], k_sl, 0.0, None, ALU.min), "vm")
                    P.wait("sc", "vm", vmin)
                    sexp = P.inc("sc", lambda e, kpar=kpar: e.activation(
                        Q_fexp[kpar], Q_fmin[kpar], AF.Exp), "sm")
                    P.wait("ve", "sm", sexp)
                    vm_fm[(g, tt)] = P.inc("ve", lambda e, k_sl=k_sl, kpar=kpar:
                        e.scalar_tensor_tensor(
                            T_kbf[kpar][:, :], k_sl, 0.0, Q_fexp[kpar],
                            ALU.max, ALU.add), "vm")
                    P.wait("sc", "tm", tm_here)
                    if prev in tm_kvmm:
                        P.wait("sc", "tm", tm_kvmm[prev])
                    def _vcopy(e, v_sl=v_sl, kpar=kpar):
                        dst = T_vaug[kpar].rearrange(
                            "p (s c) -> p s c", s=4)[:, :, 0:128]
                        src = v_sl.rearrange("p (s c) -> p s c", s=4)
                        return e.copy(dst, src)
                    sm_vaug[(g, tt)] = P.inc("sc", _vcopy, "sm")

                    i = g * NT + tt
                    if i < 64:
                        p1_pass1_chunk(i, w1T)

                _emit_kv_acc(g, NT - 1)
                if g + 2 < 4:
                    P.wait("gp", "tm", tm_kmm[(g, NT - 1)])
                    sv_wv[g + 2] = P.dma(
                        "gp", lambda e, g=g: _wslot_load(e, wvT, g + 2, g % 2),
                        f"wv{g % 2}")
                if g == 2:
                    P.wait("gp", "tm", tm_kmm[(2, NT - 1)])
                    sv_wq0_early[0] = P.dma(
                        "gp", lambda e: _wslot_load(e, wqT, 0, 0), "wv0")
                P.wait("ve", "tm", tm_kvmm[(g, NT - 1)])
                def _kv_evac(v, g=g):
                    for hh in range(4):
                        hid = g * 4 + hh
                        r = v.tensor_copy(
                            kv_loc[:, hid * KSLOT:(hid + 1) * KSLOT],
                            psB[:, hh * 512: hh * 512 + KSLOT])
                    return r
                vm_kv_evac[g] = P.inc("ve", _kv_evac, "vm")

            P.wait("ve", *sm_p1chunk[62])
            P.wait("ve", *sm_p1chunk[63])
            vm_red1 = P.inc("ve", lambda e: e.tensor_reduce(
                c_red, wsum[:, :], AX, ALU.add), "vm")
            P.wait("te", "vm", vm_red1)
            P.wait("te", "vm", vm_kv_evac[3])
            tm_t1a = P.inc("te", lambda e: e.matmul(
                psB[0:1, 0:1], c_red, ones_col_f, start=True, stop=True), "tm")
            P.wait("ve", "tm", tm_t1a)
            vm_t1b = P.inc("ve", lambda e: e.tensor_scalar(
                thr_tmp[0:1, 0:1], psB[0:1, 0:1], 0.7 / (FF * C), None, ALU.mult), "vm")
            P.wait("te", "vm", vm_t1b)
            tm_t1b = P.inc("te", lambda e: e.matmul(
                psB[:, 512:513], ones_row_f, thr_tmp[0:1, 0:1],
                start=True, stop=True), "tm")
            P.wait("ve", "tm", tm_t1b)
            P.inc("ve", lambda e: e.tensor_copy(thr1p, psB[:, 512:513]), "vm")
            vm_thr1 = P.inc("ve", lambda e: e.tensor_scalar(
                thr1n, psB[:, 512:513], -1.0, None, ALU.mult), "vm")

            HCOL = 8 * KSLOT
            sv_wq = [None] * 4
            P.wait("gp", "vm", vm_kv_evac[1])
            sv_ccin1 = P.dma("gp", lambda e: e.dma_start(
                cc_in1[:, :], kv_loc[:, 0:HCOL]), "mig")
            P.waitv("gp", sv_ccin1)
            cc1 = P.inc_default("gp", lambda e: e.collective_compute(
                "AllReduce", ALU.add, replica_groups=[list(range(NCORES))],
                ins=[cc_in1.ap().opt()], outs=[cc_out1.ap().opt()]), "cc")
            sv_wq[0] = sv_wq0_early[0]
            P.wait("gp", "cc", cc1)
            sv_kvbf1 = P.dma("gp", lambda e: e.dma_start(
                T_kvbf[:, 0:HCOL], cc_out1[:, :]), "mig")
            P.wait("gp", "vm", vm_kv_evac[3])
            sv_ccin = P.dma("gp", lambda e: e.dma_start(
                cc_in2[:, :], kv_loc[:, HCOL:KV_COLS]), "mig")
            P.waitv("gp", sv_ccin)
            cc2 = P.inc_default("gp", lambda e: e.collective_compute(
                "AllReduce", ALU.add, replica_groups=[list(range(NCORES))],
                ins=[cc_in2.ap().opt()], outs=[cc_out2.ap().opt()]), "cc")
            P.wait("gp", "tm", tm_kmm[(3, NT - 1)])
            sv_wq[1] = P.dma("gp", lambda e: _wslot_load(e, wqT, 1, 1), "wv1")
            P.wait("gp", "cc", cc2)
            sv_kvbf = P.dma("gp", lambda e: e.dma_start(
                T_kvbf[:, HCOL:KV_COLS], cc_out2[:, :]), "mig")


            tm_q01 = [0] * H
            tm_q23 = [0] * H
            vm_fm01 = [0] * H
            vm_fm23 = [0] * H
            tm_y, tm_den, tm_zb = {}, {}, {}
            sm_zln, sm_zexp = {}, {}
            vm_zbev, vm_yev = {}, {}
            sm_p2chunk = {}
            p2i = [0]

            p2_pending = {}

            def w2_chunk_issue():
                i = p2i[0]
                if i >= 64:
                    return
                p2i[0] += 1
                par = i % 2
                if i >= 2:
                    w2_chunk_consume(i - 2)
                    P.wait("gp", *sm_p2chunk[i - 2])
                else:
                    P.wait("gp", *sm_p1chunk[62 + i])
                p2_pending[i] = P.dma("gp", lambda e, i=i: e.dma_start(
                    Q_p1stg[par], w2T[i * 128:(i + 1) * 128, :]),
                    "q1a" if par == 0 else "q1b")

            def w2_chunk_consume(i):
                if i not in p2_pending or i in sm_p2chunk:
                    return
                par = i % 2
                if par == 0:
                    P.waitv("sc", p2_pending[i])
                    v = P.inc("sc", lambda e, i=i: e.activation(
                        Q_absd, Q_p1stg[0], AF.Abs, accum_out=wsum[:, i:i + 1]), "sm")
                    sm_p2chunk[i] = ("sm", v)
                else:
                    P.waitv("ve", p2_pending[i])
                    v = P.inc("ve", lambda e, i=i: e.tensor_reduce(
                        wsum[:, i:i + 1], Q_p1stg[1], AX, ALU.add,
                        apply_absolute_value=True), "vm")
                    sm_p2chunk[i] = ("vm", v)

            def _fm_pair(h, ts0):
                qpar = h % 2
                qo = (h % 2) * 1024
                out = 0
                for ts in (ts0, ts0 + 1):
                    fpar = ts % 2
                    vmin = P.inc("ve", lambda e, ts=ts, fpar=fpar: e.tensor_scalar(
                        Q_fmin[fpar],
                        psA[:, qo + (ts % 2) * 512: qo + ((ts % 2) + 1) * 512],
                        0.0, None, ALU.min), "vm")
                    P.wait("sc", "vm", vmin)
                    sexp = P.inc("sc", lambda e, fpar=fpar: e.activation(
                        Q_fexp[fpar], Q_fmin[fpar], AF.Exp), "sm")
                    P.wait("ve", "sm", sexp)
                    out = P.inc("ve", lambda e, ts=ts, fpar=fpar, qpar=qpar:
                        e.scalar_tensor_tensor(
                            Q_qT[qpar][:, ts * 512:(ts + 1) * 512],
                            psA[:, qo + (ts % 2) * 512: qo + ((ts % 2) + 1) * 512],
                            0.0, Q_fexp[fpar], ALU.max, ALU.add), "vm")
                return out

            def _q_half(h, ts0):
                g4, hh = divmod(h, 4)
                qo = (h % 2) * 1024
                def _q(t, g4=g4, hh=hh, ts0=ts0):
                    for ct in range(CT):
                        lhs = S_wq[g4 % 2][:, ct * 512 + hh * 128:
                                           ct * 512 + hh * 128 + 128]
                        for j, ts in enumerate((ts0, ts0 + 1)):
                            mm = t.matmul(
                                psA[:, qo + j * 512: qo + (j + 1) * 512], lhs,
                                A_xT[:, ct * TC + ts * 512: ct * TC + (ts + 1) * 512],
                                start=(ct == 0), stop=(ct == CT - 1))
                    return mm
                return P.inc("te", _q, "tm")

            for step in range(H + 1):
                h = step
                hp = step - 1
                if h < H:
                    g4, hh = divmod(h, 4)
                    P.waitv("te", sv_wq[g4])
                    if h == 0:
                        P.wait("te", "vm", vm_fm[(3, NT - 1)])
                        P.wait("te", "vm", vm_thr1)
                    elif h == 1:
                        P.wait("te", "sm", sm_vaug[(3, NT - 1)])
                    else:
                        P.wait("te", "vm", vm_zbev[(h - 2, 3)])
                    tm_q01[h] = _q_half(h, 0)

                if hp >= 0:
                    qpar = hp % 2
                    if hp == 0:
                        P.waitv("te", sv_kvbf1)
                    if hp == 8:
                        P.waitv("te", sv_kvbf)
                    P.wait("te", "vm", vm_fm23[hp])
                    for ts in range(4):
                        sl = ts % 2
                        prev = (hp, ts - 2) if ts >= 2 else (hp - 1, 2 + ts)
                        if prev in vm_yev:
                            P.wait("te", "vm", vm_yev[prev])
                        tm_y[(hp, ts)] = P.inc("te", lambda e, hp=hp, ts=ts, sl=sl,
                                               qpar=qpar: e.matmul(
                            psB[:, sl * 512:(sl + 1) * 512],
                            T_kvbf[:, hp * KSLOT: hp * KSLOT + 128],
                            Q_qT[qpar][:, ts * 512:(ts + 1) * 512],
                            start=True, stop=True), "tm")
                        if prev in sm_zln:
                            P.wait("te", "sm", sm_zln[prev])
                        tm_den[(hp, ts)] = P.inc("te", lambda e, hp=hp, ts=ts, sl=sl,
                                                 qpar=qpar: e.matmul(
                            psB[0:1, 1024 + sl * 512: 1024 + (sl + 1) * 512],
                            T_kvbf[:, hp * KSLOT + 128: hp * KSLOT + 129],
                            Q_qT[qpar][:, ts * 512:(ts + 1) * 512],
                            start=True, stop=True), "tm")
                        P.wait("sc", "tm", tm_den[(hp, ts)])
                        if (hp - 1, ts) in tm_zb:
                            P.wait("sc", "tm", tm_zb[(hp - 1, ts)])
                        sm_zln[(hp, ts)] = P.inc("sc", lambda e, sl=sl: e.activation(
                            z_ln, psB[0:1, 1024 + sl * 512: 1024 + (sl + 1) * 512],
                            AF.Ln, bias=eps_z6), "sm")
                        sm_zexp[(hp, ts)] = P.inc("sc", lambda e, ts=ts: e.activation(
                            z_bf[ts], z_ln, AF.Exp, scale=-1.0), "sm")

                if h < H:
                    P.wait("ve", "tm", tm_q01[h])
                    if h == 0:
                        P.waitv("ve", sv_ccin1)
                        P.waitv("ve", sv_ccin)
                    if h >= 2:
                        P.wait("ve", "tm", tm_den[(h - 2, 3)])
                    vm_fm01[h] = _fm_pair(h, 0)

                if hp >= 0:
                    zo = (hp % 2) * 1024
                    for ts in range(4):
                        sl = ts % 2
                        prev = (hp, ts - 2) if ts >= 2 else (hp - 1, 2 + ts)
                        if prev in vm_zbev:
                            P.wait("te", "vm", vm_zbev[prev])
                        P.wait("te", "sm", sm_zexp[(hp, ts)])
                        tm_zb[(hp, ts)] = P.inc("te", lambda e, sl=sl, ts=ts, zo=zo:
                            e.matmul(psA[:, zo + sl * 512: zo + (sl + 1) * 512],
                                     ones_row_b, z_bf[ts],
                                     start=True, stop=True), "tm")
                        P.wait("ve", "tm", tm_zb[(hp, ts)])
                        vm_zbev[(hp, ts)] = P.inc("ve", lambda e, sl=sl, zo=zo:
                            e.tensor_copy(
                                Q_zb[sl], psA[:, zo + sl * 512: zo + (sl + 1) * 512]),
                            "vm")
                        P.wait("ve", "tm", tm_y[(hp, ts)])
                        vm_yev[(hp, ts)] = P.inc("ve", lambda e, hp=hp, ts=ts, sl=sl:
                            e.tensor_tensor(
                                B_yT[:, hp * TC + ts * 512: hp * TC + (ts + 1) * 512],
                                psB[:, sl * 512:(sl + 1) * 512],
                                Q_zb[sl], ALU.mult), "vm")

                if h < H:
                    P.wait("te", "vm", vm_fm01[h])
                    tm_q23[h] = _q_half(h, 2)
                    P.wait("ve", "tm", tm_q23[h])
                    vm_fm23[h] = _fm_pair(h, 2)

                    g4, hh = divmod(h, 4)
                    if hh == 3 and g4 + 2 <= 3:
                        P.wait("gp", "tm", tm_q23[h])
                        sv_wq[g4 + 2] = P.dma(
                            "gp", lambda e, g4=g4: _wslot_load(e, wqT, g4 + 2, g4 % 2),
                            f"wv{g4 % 2}")
                    w2_chunk_issue()
                    w2_chunk_issue()

            tm_q = tm_q23
            vm_p2 = P.c["vm"]

            P.wait("gp", "tm", tm_q[H - 1])
            sv_wo = []
            for ct in range(CT):
                sv_wo.append(P.dma("gp", lambda e, ct=ct: e.dma_start(
                    A_woT[:, ct * TC:(ct + 1) * TC],
                    woT[ct * 128:(ct + 1) * 128, :]), f"wo{ct % 2}"))

            sv_xres = {}
            P.wait("sy", "tm", tm_q[H - 1])
            for tt in range(2):
                sv_xres[tt] = P.dma("sy", lambda e, tt=tt: e.dma_start(
                    S_xres[tt % 2], x_in[tt * 128:(tt + 1) * 128, :]), f"xr{tt % 2}")
            tm_p3 = {}
            vm_p3pre = {}
            sm_x2 = {}
            sv_x2 = {}
            for tt in range(NT):
                par = tt % 2
                ps = psA if par == 0 else psB
                if tt >= 2:
                    P.wait("te", "vm", vm_p3pre[tt - 2])
                if tt == 0:
                    P.wait("te", "vm", vm_p2)
                    for ct in range(CT):
                        P.waitv("te", sv_wo[ct])
                        def _p3c(t, ct=ct, ps=ps):
                            lhs = B_yT[:, ct * TC: ct * TC + 128]
                            for ds in range(4):
                                mm = t.matmul(
                                    ps[:, ds * 512:(ds + 1) * 512], lhs,
                                    A_woT[:, ct * TC + ds * 512:
                                          ct * TC + (ds + 1) * 512],
                                    start=(ct == 0), stop=(ct == CT - 1))
                            return mm
                        tm_p3[tt] = P.inc("te", _p3c, "tm")
                else:
                    def _p3_mms(t, tt=tt, ps=ps):
                        for ct in range(CT):
                            lhs = B_yT[:, ct * TC + tt * 128: ct * TC + tt * 128 + 128]
                            for ds in range(4):
                                mm = t.matmul(
                                    ps[:, ds * 512:(ds + 1) * 512], lhs,
                                    A_woT[:, ct * TC + ds * 512:
                                          ct * TC + (ds + 1) * 512],
                                    start=(ct == 0), stop=(ct == CT - 1))
                        return mm
                    tm_p3[tt] = P.inc("te", _p3_mms, "tm")

                P.wait("ve", "tm", tm_p3[tt])
                P.waitv("ve", sv_xres[tt])
                if tt >= 2:
                    P.wait("ve", "sm", sm_x2[tt - 2])
                vm_pre = P.inc("ve", lambda e, ps=ps, par=par: e.scalar_tensor_tensor(
                    Q_pre[par], ps[:, :], 0.0, S_xres[par],
                    ALU.bypass, ALU.add, accum_out=ln_s1[par]), "vm")
                vm_p3pre[tt] = vm_pre
                if tt + 2 < NT:
                    P.wait("sy", "vm", vm_pre)
                    sv_xres[tt + 2] = P.dma("sy", lambda e, tt=tt, par=par: e.dma_start(
                        S_xres[par], x_in[(tt + 2) * 128:(tt + 3) * 128, :]),
                        f"xr{par}")
                P.wait("sc", "vm", vm_pre)
                sm_sq3 = P.inc("sc", lambda e, par=par: e.activation(
                    T_sq, Q_pre[par], AF.Square, accum_out=ln_s2[par]), "sm")
                P.wait("ve", "sm", sm_sq3)
                P.inc("ve", lambda e, par=par: e.tensor_scalar(
                    ln_mean[par], ln_s1[par], 1.0 / C, None, ALU.mult), "vm")
                P.inc("ve", lambda e, par=par: e.tensor_scalar(
                    ln_var[par], ln_s2[par], 1.0 / C, None, ALU.mult), "vm")
                P.inc("ve", lambda e, par=par: e.scalar_tensor_tensor(
                    ln_nm[par], ln_mean[par], -1.0, ln_mean[par],
                    ALU.mult, ALU.mult), "vm")
                vm_var = P.inc("ve", lambda e, par=par: e.tensor_tensor(
                    ln_var[par], ln_var[par], ln_nm[par], ALU.add), "vm")
                P.wait("sc", "vm", vm_var)
                sm_sd = P.inc("sc", lambda e, par=par: e.activation(
                    ln_sd[par], ln_var[par], AF.Sqrt, bias=eps_ln), "sm")
                P.wait("ve", "sm", sm_sd)
                P.inc("ve", lambda e, par=par: e.reciprocal(
                    ln_rstd[par], ln_sd[par]), "vm")
                vm_coef = P.inc("ve", lambda e, par=par: e.scalar_tensor_tensor(
                    ln_nm[par], ln_mean[par], -1.0, ln_rstd[par],
                    ALU.mult, ALU.mult), "vm")
                P.wait("sc", "vm", vm_coef)
                if tt >= 2:
                    P.waitv("sc", sv_x2[tt - 2])
                sm_x2[tt] = P.inc("sc", lambda e, par=par: e.activation(
                    S_x2bf[par], Q_pre[par], AF.Identity,
                    bias=ln_nm[par], scale=ln_rstd[par]), "sm")
                P.wait("gp", "sm", sm_x2[tt])
                sv_x2[tt] = P.dma("gp", lambda e, tt=tt, par=par: e.dma_start(
                    x2s[tt * 128:(tt + 1) * 128, :], S_x2bf[par]), f"x2{par}")

                w2_chunk_issue()
                w2_chunk_issue()
                w2_chunk_issue()
            while p2i[0] < 64:
                w2_chunk_issue()
                w2_chunk_issue()
            for i in range(64):
                w2_chunk_consume(i)

            sm_w1sg = {}
            vm_w1add = {}
            vm_w1q = [0] * 16
            tm_mm1 = {}
            sm_gelu = {}
            sv_ht = {}
            sm_w2sg = {}
            vm_w2add = {}
            sv_w2q = {}
            w1q_sv = {}

            def w1q_issue(qgrp, ct):
                i = qgrp * CT + ct
                b = i % 4
                if i >= 4:
                    P.wait("sy", *sm_w1sg[i - 4])
                else:
                    P.wait("sy", *sm_p2chunk[62 + i // 2])
                w1q_sv[i] = P.dma("sy", lambda e, qgrp=qgrp, ct=ct, b=b: e.dma_start(
                    Q_q2stg[b],
                    w1T[ct * 128:(ct + 1) * 128,
                        qgrp * 512:(qgrp + 1) * 512]),
                    ("p2a", "p2b", "p4a", "p4b")[b])

            def w1q_consume(qgrp, ct):
                i = qgrp * CT + ct
                b = i % 4
                par = i % 2
                src = Q_q2stg[b]
                if ct == 0:
                    if qgrp >= 4:
                        P.wait("ve", "tm", tm_mm1[(qgrp - 4) * 4 + 3])
                    else:
                        P.wait("ve", "tm", tm_p3[NT - 1])
                dstB = B_w1q[qgrp % 4][:, ct * 512:(ct + 1) * 512]
                if par == 0:
                    P.waitv("sc", w1q_sv[i])
                    if i == 0:
                        P.wait("sc", "vm", vm_thr1)
                    if i >= 2:
                        P.wait("sc", "vm", vm_w1add[i - 2])
                    P.inc("sc", lambda e, src=src: e.activation(
                        Q_sgnA[:, 0:512], src, AF.Sign, bias=thr1n), "sm")
                    s = P.inc("sc", lambda e, src=src: e.activation(
                        Q_sgnB[:, 0:512], src, AF.Sign, bias=thr1p), "sm")
                    sm_w1sg[i] = ("sm", s)
                    P.wait("ve", "sm", s)
                    vm_w1add[i] = P.inc("ve", lambda e, dstB=dstB: e.tensor_tensor(
                        dstB, Q_sgnA[:, 0:512], Q_sgnB[:, 0:512], ALU.add), "vm")
                else:
                    P.waitv("ve", w1q_sv[i])
                    if i == 1:
                        P.wait("ve", "vm", vm_thr1)
                    P.inc("ve", lambda e, src=src: e.tensor_scalar(
                        Q_vg[0][:, 0:512], src,
                        thr1p, 2.0, ALU.is_ge, ALU.mult), "vm")
                    P.inc("ve", lambda e, src=src: e.tensor_scalar(
                        Q_vg[1][:, 0:512], src,
                        thr1n, 2.0, ALU.is_le, ALU.mult), "vm")
                    v = P.inc("ve", lambda e, dstB=dstB: e.tensor_tensor(
                        dstB, Q_vg[0][:, 0:512], Q_vg[1][:, 0:512],
                        ALU.subtract), "vm")
                    sm_w1sg[i] = ("vm", v)
                    vm_w1add[i] = v
                if ct == CT - 1:
                    vm_w1q[qgrp] = vm_w1add[i]

            for ct in range(4):
                w1q_issue(0, ct)

            P.wait("sy", "tm", tm_p3[NT - 1])
            for tt in range(NT):
                P.waitv("sy", sv_x2[tt])
            P.wait("sc", "tm", tm_p3[NT - 1])
            P.waitv("sc", sv_x2[NT - 2])
            P.waitv("sc", sv_x2[NT - 1])
            sv_x2T = []
            def x2T_load(ct):
                sv_x2T.append(P.dma("sy", lambda e, ct=ct: e.dma_start(
                    A_x2T[:, ct * TC:(ct + 1) * TC],
                    x2s[:, ct * 128:(ct + 1) * 128], transpose=True), f"x2T{ct % 4}"))
            for ct in range(4):
                x2T_load(ct)

            P.wait("ve", *sm_p2chunk[62])
            P.wait("ve", *sm_p2chunk[63])
            vm_red2 = P.inc("ve", lambda e: e.tensor_reduce(
                c_red, wsum[:, :], AX, ALU.add), "vm")
            P.wait("te", "vm", vm_red2)
            P.wait("te", "vm", vm_p3pre[NT - 1])
            tm_t2a = P.inc("te", lambda e: e.matmul(
                psB[0:1, 0:1], c_red, ones_col_f, start=True, stop=True), "tm")
            P.wait("ve", "tm", tm_t2a)
            vm_t2b = P.inc("ve", lambda e: e.tensor_scalar(
                thr_tmp[0:1, 0:1], psB[0:1, 0:1], 0.7 / (FF * C), None,
                ALU.mult), "vm")
            P.wait("te", "vm", vm_t2b)
            tm_t2b = P.inc("te", lambda e: e.matmul(
                psB[:, 512:513], ones_row_f, thr_tmp[0:1, 0:1],
                start=True, stop=True), "tm")
            P.wait("ve", "tm", tm_t2b)
            P.inc("ve", lambda e: e.tensor_copy(thr2p, psB[:, 512:513]), "vm")
            vm_thr2 = P.inc("ve", lambda e: e.tensor_scalar(
                thr2n, psB[:, 512:513], -1.0, None, ALU.mult), "vm")

            def w2_quant_ft(ft):
                for dh2 in range(2):
                    j = ft * 2 + dh2
                    par = j % 2
                    if j >= 2:
                        P.wait("sy", "sm", sm_w2sg[j - 2])
                    sv_here = P.dma("sy", lambda e, ft=ft, dh2=dh2: e.dma_start(
                        Q_w2stg[par],
                        w2T[ft * 128:(ft + 1) * 128,
                            dh2 * 1024:(dh2 + 1) * 1024]),
                        "p3a" if par == 0 else "p3b")
                    P.waitv("sc", sv_here)
                    if j == 0:
                        P.wait("sc", "vm", vm_thr2)
                    if j >= 1:
                        P.wait("sc", "vm", vm_w2add[j - 1])
                    P.inc("sc", lambda e: e.activation(
                        Q_sgn2A, Q_w2stg[par], AF.Sign, bias=thr2n), "sm")
                    sm_w2sg[j] = P.inc("sc", lambda e: e.activation(
                        Q_sgn2B, Q_w2stg[par], AF.Sign, bias=thr2p), "sm")
                    P.wait("ve", "sm", sm_w2sg[j])
                    if j >= 2:
                        P.waitv("ve", sv_w2q[j - 2])
                    vm_w2add[j] = P.inc("ve", lambda e, par=par: e.tensor_tensor(
                        Q_w2out[par], Q_sgn2A, Q_sgn2B, ALU.add), "vm")
                    P.wait("gp", "vm", vm_w2add[j])
                    sv_w2q[j] = P.dma("gp", lambda e, ft=ft, dh2=dh2, par=par:
                        e.dma_start(
                            w2qs[ft * 128:(ft + 1) * 128,
                                 dh2 * 1024:(dh2 + 1) * 1024], Q_w2out[par]),
                        f"wqs{par}")

            for fb in range(NFB):
                fgrp, fi = divmod(fb, 4)
                par = fb % 2
                ps = psA if par == 0 else psB
                if fb == 0:
                    for ct in range(CT):
                        w1q_consume(0, ct)
                        if ct + 4 < CT:
                            w1q_issue(0, ct + 4)
                        if ct >= 4:
                            x2T_load(ct)
                    for ct in range(CT):
                        w1q_issue(1, ct)
                        w1q_consume(1, ct)
                elif fb >= 4:
                    g_next = fb // 4 + 1
                    if g_next <= 15:
                        fi_ = fb % 4
                        for j in range(4):
                            w1q_issue(g_next, 4 * fi_ + j)
                            w1q_consume(g_next, 4 * fi_ + j)
                if 2 <= fb:
                    w2_quant_ft(fb - 2)
                if fb == NFB - 1:
                    w2_quant_ft(62)
                    w2_quant_ft(63)

                if fb != 0:
                    P.wait("te", "vm", vm_w1q[fgrp])
                if fb == 1:
                    P.wait("te", "vm", vm_thr2)
                if fb >= 2:
                    P.wait("te", "sm", sm_gelu[fb - 2])
                if fb == 0:
                    P.wait("te", "vm", vm_p3pre[NT - 2])
                    for ct in range(CT):
                        P.waitv("te", sv_x2T[ct])
                        P.wait("te", "vm", vm_w1add[ct])
                        def _mm1c(t, ct=ct, fgrp=fgrp, fi=fi, ps=ps):
                            lhs = B_w1q[fgrp % 4][:, ct * 512 + fi * 128:
                                                  ct * 512 + fi * 128 + 128]
                            for ts in range(4):
                                mm = t.matmul(
                                    ps[:, ts * 512:(ts + 1) * 512], lhs,
                                    A_x2T[:, ct * TC + ts * 512:
                                          ct * TC + (ts + 1) * 512],
                                    start=(ct == 0), stop=(ct == CT - 1))
                            return mm
                        tm_mm1[fb] = P.inc("te", _mm1c, "tm")
                else:
                    def _mm1(t, fgrp=fgrp, fi=fi, ps=ps):
                        for ct in range(CT):
                            lhs = B_w1q[fgrp % 4][:, ct * 512 + fi * 128:
                                                  ct * 512 + fi * 128 + 128]
                            for ts in range(4):
                                mm = t.matmul(
                                    ps[:, ts * 512:(ts + 1) * 512], lhs,
                                    A_x2T[:, ct * TC + ts * 512:
                                          ct * TC + (ts + 1) * 512],
                                    start=(ct == 0), stop=(ct == CT - 1))
                        return mm
                    tm_mm1[fb] = P.inc("te", _mm1, "tm")

                P.wait("sc", "tm", tm_mm1[fb])
                if fb == 0:
                    P.waitv("sc", sv_b1)
                if fb >= 2:
                    P.waitv("sc", sv_ht[fb - 2])
                sm_gelu[fb] = P.inc("sc", lambda e, fb=fb, par=par: e.activation(
                    S_hstage[par], (psA if par == 0 else psB)[:, :], AF.Gelu,
                    bias=c_b1[:, fb:fb + 1], scale=0.5), "sm")
                P.wait("gp", "sm", sm_gelu[fb])
                sv_ht[fb] = P.dma("gp", lambda e, fb=fb, par=par: e.dma_start(
                    hts[fb * 128:(fb + 1) * 128, :], S_hstage[par]), f"ht{fb % 2}")

            tmA_m2, tmB_m2 = {}, {}
            vm_m2pre = {}
            sm_m2done = {}
            sv_pre0 = {}
            slot_tm = [0, 0]
            so_hist = []
            vm_b2b_prev = [0]

            def load_w2half(half):
                svs = []
                for blk in range(8):
                    ftb = blk * 8
                    dst = B_w2lo if blk < 4 else A_w2hi
                    off_el = (ftb % 32) * 1024
                    def _ld(e, ftb=ftb, dst=dst, off_el=off_el, half=half):
                        d = dst[:, off_el: off_el + 8 * 1024].rearrange(
                            "p (f d) -> p f d", f=8)
                        s = w2qs[ftb * 128:(ftb + 8) * 128,
                                 half * 1024:(half + 1) * 1024].rearrange(
                            "(f p) d -> p f d", p=128)
                        return e.dma_start(d, s)
                    svs.append(P.dma("sy", _ld, f"wh{blk % 2}"))
                return svs

            P.wait("sy", "tm", tm_mm1[NFB - 1])
            P.waitv("sy", sv_w2q[126])
            P.waitv("sy", sv_w2q[127])
            for half in range(2):
                if half == 1:
                    P.wait("sy", "tm", tmB_m2[(0, NT - 1)])
                sv_w2h = load_w2half(half)
                sv_b2 = P.dma("sy", lambda e, half=half: e.dma_start(
                    Q_b2row, b2r_in[0:1, half * 1024:(half + 1) * 1024]), "mi")
                P.waitv("te", sv_b2)
                if half == 0:
                    P.wait("te", "sm", sm_gelu[NFB - 1])
                else:
                    P.wait("te", "vm", vm_b2b_prev[0])
                def _b2mm(t):
                    for i in range(2):
                        mm = t.matmul(psB[:, i * 512:(i + 1) * 512],
                                      ones_row_f, Q_b2row[0:1, i * 512:(i + 1) * 512],
                                      start=True, stop=True)
                    return mm
                tm_b2 = P.inc("te", _b2mm, "tm")
                P.wait("ve", "tm", tm_b2)
                vm_b2b = P.inc("ve", lambda e: e.tensor_copy(Q_b2b, psB[:, 0:1024]), "vm")
                vm_b2b_prev[0] = vm_b2b

                def ht_load(e, pair, slot):
                    d = S_ht2[slot].rearrange("p (f t) -> p f t", f=32)
                    fb0 = slot * 32
                    s = hts[fb0 * 128:(fb0 + 32) * 128,
                            pair * 256:(pair + 1) * 256].rearrange(
                        "(f p) t -> p f t", p=128)
                    return e.dma_start(d, s)
                if half == 0:
                    P.waitv("gp", sv_ht[NFB - 2])
                    P.waitv("gp", sv_ht[NFB - 1])
                P.wait("gp", "tm", slot_tm[0])
                cd_next = [P.dma("gp", lambda e: ht_load(e, 0, 0), "hs0"), None]
                P.wait("gp", "tm", slot_tm[1])
                cd_next[1] = P.dma("gp", lambda e: ht_load(e, 0, 1), "hs1")

                for pair in range(NT // 2):
                    lo, hi = 2 * pair, 2 * pair + 1
                    acc_lo = psA[:, 0:1024]
                    acc_hi = psA[:, 1024:2048]
                    cd0, cd1 = cd_next

                    vm_r2t = {}
                    for tt in (lo, hi):
                        if (half, tt - 2) in vm_m2pre:
                            P.wait("sy", "vm", vm_m2pre[(half, tt - 2)])
                        elif half == 1 and tt < 2:
                            P.wait("sy", "vm", vm_m2pre[(0, NT - 2 + tt)])
                        sv_x2r = P.dma("sy", lambda e, tt=tt, half=half: e.dma_start(
                            T_x2res[tt % 2],
                            x2s[tt * 128:(tt + 1) * 128,
                                half * 1024:(half + 1) * 1024]), f"x2r{tt % 2}")
                        P.waitv("ve", sv_x2r)
                        P.wait("ve", "vm", vm_b2b)
                        vm_r2t[tt] = P.inc("ve", lambda e, tt=tt: e.tensor_tensor(
                            Q_r2[tt % 2], T_x2res[tt % 2], Q_b2b, ALU.add), "vm")

                    sv_p0rd = {}
                    if half == 1:
                        for tt in (lo, hi):
                            if (1, tt - 2) in sm_m2done:
                                P.wait("sy", "sm", sm_m2done[(1, tt - 2)])
                            P.waitv("sy", sv_pre0[tt])
                            sv_p0rd[tt] = P.dma("sy", lambda e, tt=tt: e.dma_start(
                                Q_p0rd[tt % 2], pre0s[tt * 128:(tt + 1) * 128, :]),
                                f"p0r{tt % 2}")

                    if half == 0 and pair == 0:
                        P.wait("te", "sm", sm_gelu[NFB - 2])
                        for sv in sv_w2h:
                            P.waitv("te", sv)
                    if half == 1 and pair == 0:
                        for sv in sv_w2h:
                            P.waitv("te", sv)
                    P.waitv("te", cd0)
                    def _mpart(t, acc, slot, toff, rhs_t, start, stop):
                        for ft in range(32):
                            lhs = S_ht2[slot][:, ft * 256 + toff: ft * 256 + toff + 128]
                            for i in range(2):
                                mm = t.matmul(
                                    acc[:, i * 512:(i + 1) * 512], lhs,
                                    rhs_t[:, ft * 1024 + i * 512:
                                          ft * 1024 + (i + 1) * 512],
                                    start=(start and ft == 0),
                                    stop=(stop and ft == 31))
                        return mm
                    if (half, lo - 2) in vm_m2pre:
                        P.wait("te", "vm", vm_m2pre[(half, lo - 2)])
                    elif half == 1 and lo < 2:
                        P.wait("te", "vm", vm_m2pre[(0, NT - 2 + lo)])
                    P.inc("te", lambda e: _mpart(e, acc_lo, 0, 0, B_w2lo, True, False),
                          "tm")
                    if (half, hi - 2) in vm_m2pre:
                        P.wait("te", "vm", vm_m2pre[(half, hi - 2)])
                    elif half == 1 and hi < 2:
                        P.wait("te", "vm", vm_m2pre[(0, NT - 2 + hi)])
                    tmA = P.inc("te", lambda e: _mpart(e, acc_hi, 0, 128, B_w2lo,
                                                       True, False), "tm")
                    slot_tm[0] = tmA
                    P.waitv("te", cd1)
                    tm_lo = P.inc("te", lambda e: _mpart(e, acc_lo, 1, 0, A_w2hi,
                                                         False, True), "tm")
                    tmA_m2[(half, lo)] = tm_lo
                    tm_hi = P.inc("te", lambda e: _mpart(e, acc_hi, 1, 128, A_w2hi,
                                                         False, True), "tm")
                    tmA_m2[(half, hi)] = tm_hi
                    tmB_m2[(half, lo)] = tm_lo
                    tmB_m2[(half, hi)] = tm_hi
                    slot_tm[1] = tm_hi

                    if pair + 1 < NT // 2:
                        P.wait("gp", "tm", slot_tm[0])
                        cd_next = [P.dma("gp", lambda e, pair=pair: ht_load(
                            e, pair + 1, 0), "hs0"), None]
                        P.wait("gp", "tm", slot_tm[1])
                        cd_next[1] = P.dma("gp", lambda e, pair=pair: ht_load(
                            e, pair + 1, 1), "hs1")

                    for tt in (lo, hi):
                        acc = acc_lo if tt == lo else acc_hi
                        P.wait("ve", "tm", tmB_m2[(half, tt)])
                        P.wait("ve", "vm", vm_r2t[tt])
                        if (half, tt - 2) in sm_m2done:
                            P.wait("ve", "sm", sm_m2done[(half, tt - 2)])
                            if half == 0:
                                P.waitv("ve", sv_pre0[tt - 2])
                        elif half == 1 and tt < 2:
                            P.wait("ve", "sm", sm_m2done[(0, NT - 2 + tt)])
                            P.waitv("ve", sv_pre0[NT - 2 + tt])
                        stat_s1 = s1a[:, tt:tt + 1] if half == 0 else ln_s1[tt % 2]
                        vm_pre = P.inc("ve", lambda e, acc=acc, tt=tt, stat_s1=stat_s1:
                            e.scalar_tensor_tensor(
                                Q_m2pre[tt % 2], acc, 0.5, Q_r2[tt % 2],
                                ALU.mult, ALU.add, accum_out=stat_s1), "vm")
                        vm_m2pre[(half, tt)] = vm_pre
                        stat_s2 = s2a[:, tt:tt + 1] if half == 0 else ln_s2[tt % 2]
                        P.wait("sc", "vm", vm_pre)
                        sm_sq2 = P.inc("sc", lambda e, tt=tt, stat_s2=stat_s2:
                            e.activation(T_sq[:, 0:DHALF], Q_m2pre[tt % 2],
                                         AF.Square, accum_out=stat_s2), "sm")

                        if half == 0:
                            P.wait("gp", "vm", vm_pre)
                            sv_pre0[tt] = P.dma("gp", lambda e, tt=tt: e.dma_start(
                                pre0s[tt * 128:(tt + 1) * 128, :], Q_m2pre[tt % 2]),
                                f"p0{tt % 2}")
                            sm_m2done[(half, tt)] = sm_sq2
                        else:
                            par2 = tt % 2
                            P.wait("ve", "sm", sm_sq2)
                            P.inc("ve", lambda e, tt=tt, par2=par2: e.tensor_tensor(
                                ln_s1[par2], ln_s1[par2], s1a[:, tt:tt + 1],
                                ALU.add), "vm")
                            P.inc("ve", lambda e, tt=tt, par2=par2: e.tensor_tensor(
                                ln_s2[par2], ln_s2[par2], s2a[:, tt:tt + 1],
                                ALU.add), "vm")
                            P.inc("ve", lambda e, par2=par2: e.tensor_scalar(
                                ln_mean[par2], ln_s1[par2], 1.0 / C, None,
                                ALU.mult), "vm")
                            P.inc("ve", lambda e, par2=par2: e.tensor_scalar(
                                ln_var[par2], ln_s2[par2], 1.0 / C, None,
                                ALU.mult), "vm")
                            P.inc("ve", lambda e, par2=par2: e.scalar_tensor_tensor(
                                ln_nm[par2], ln_mean[par2], -1.0, ln_mean[par2],
                                ALU.mult, ALU.mult), "vm")
                            vm_v = P.inc("ve", lambda e, par2=par2: e.tensor_tensor(
                                ln_var[par2], ln_var[par2], ln_nm[par2],
                                ALU.add), "vm")
                            P.wait("sc", "vm", vm_v)
                            sm_sd2 = P.inc("sc", lambda e, par2=par2: e.activation(
                                ln_sd[par2], ln_var[par2], AF.Sqrt,
                                bias=eps_ln), "sm")
                            P.wait("ve", "sm", sm_sd2)
                            P.inc("ve", lambda e, par2=par2: e.reciprocal(
                                ln_rstd[par2], ln_sd[par2]), "vm")
                            vm_cf = P.inc("ve", lambda e, par2=par2:
                                e.scalar_tensor_tensor(
                                    ln_nm[par2], ln_mean[par2], -1.0, ln_rstd[par2],
                                    ALU.mult, ALU.mult), "vm")
                            P.wait("sc", "vm", vm_cf)
                            P.waitv("sc", sv_p0rd[tt])
                            sm_last = 0
                            for piece in range(2):
                                src = Q_p0rd[tt % 2] if piece == 0 else Q_m2pre[tt % 2]
                                for seg in range(2):
                                    opar = (piece * 2 + seg) % 2
                                    if len(so_hist) >= 2:
                                        P.waitv("sc", so_hist[-2])
                                    sm_ap = P.inc("sc", lambda e, src=src, seg=seg,
                                                  opar=opar, par2=par2: e.activation(
                                        T_ost[opar], src[:, seg * 512:(seg + 1) * 512],
                                        AF.Identity, bias=ln_nm[par2],
                                        scale=ln_rstd[par2]), "sm")
                                    sm_last = sm_ap
                                    P.wait("sy", "sm", sm_ap)
                                    col0 = piece * 1024 + seg * 512
                                    soh = P.dma("sy", lambda e, tt=tt, col0=col0,
                                                opar=opar: e.dma_start(
                                        out_t[tt * 128:(tt + 1) * 128,
                                              col0:col0 + 512],
                                        T_ost[opar]), f"so{opar}")
                                    so_hist.append(soh)
                            sm_m2done[(half, tt)] = sm_last

            P.wait("sy", "so0", P.c["so0"])
            P.wait("sy", "so1", P.c["so1"])

        def mk(eng_name):
            def body(eng):
                prog(Prog(eng_name, eng, sems))
            return body
        block.sync(mk("sy"))
        block.gpsimd(mk("gp"))
        block.vector(mk("ve"))
        block.scalar(mk("sc"))
        block.tensor(mk("te"))

    return nc


_NC_CACHE = []


def kernel(**inputs):
    _ = inputs.get("L0_attn_cos"), inputs.get("L0_attn_sin")
    x = np.asarray(inputs["L0_attn_x"], dtype=np.float32).reshape(T, C)
    wqT = np.ascontiguousarray(np.asarray(inputs["L0_attn_Wq"], np.float32).T)
    wkT = np.ascontiguousarray(np.asarray(inputs["L0_attn_Wk"], np.float32).T)
    wvT = np.ascontiguousarray(np.asarray(inputs["L0_attn_Wv"], np.float32).T)
    woT = np.ascontiguousarray(np.asarray(inputs["L0_attn_Wo"], np.float32).T)
    w1T = np.ascontiguousarray(np.asarray(inputs["L0_ffn_w1"], np.float32).T)
    w2T = np.ascontiguousarray(np.asarray(inputs["L0_ffn_w2"], np.float32).T)
    b1c = np.ascontiguousarray(
        np.asarray(inputs["L0_ffn_b1"], np.float32).reshape(NFB, 128).T)
    b2r = np.ascontiguousarray(
        np.asarray(inputs["L0_ffn_b2"], np.float32).reshape(1, C))

    nc = build_nc()
    in_maps = []
    for c in range(NCORES):
        xs = np.ascontiguousarray(x[c * TC:(c + 1) * TC, :])
        in_maps.append(dict(
            x=xs, xT=np.ascontiguousarray(xs.T),
            wqT=wqT, wkT=wkT, wvT=wvT, woT=woT,
            w1T=w1T, w2T=w2T, b1c=b1c, b2r=b2r))
    res = run_bass_kernel_spmd(nc, in_maps, core_ids=list(range(NCORES)))
    out = np.concatenate([res.results[c]["out"] for c in range(NCORES)], axis=0)
    return out.reshape(1, T, C)


if __name__ == "__main__":
    build_nc()
    print("build ok")


# version 17 (final)
# speedup vs baseline: 1.0728x; 1.0728x over previous
import contextlib
import numpy as np
import concourse.bass as bass
import concourse.mybir as mybir
from concourse.bass_utils import run_bass_kernel_spmd

F32 = mybir.dt.float32
BF16 = mybir.dt.bfloat16
U8 = mybir.dt.uint8
AF = mybir.ActivationFunctionType
ALU = mybir.AluOpType
AX = mybir.AxisListType.X

NCORES = 8
T = 16384
C = 2048
TC = T // NCORES
NT = TC // 128
CT = C // 128
H = 16
FF = 8192
NFB = FF // 128
DHALF = 1024

KSLOT = 129
KV_COLS = H * KSLOT
VSLOT = 144

A_OFF = 0
B_OFF = 65536
S_OFF = 131072
Q_OFF = 163840
T_OFF = 196608
C_OFF = 209600
ARENA_BYTES = 212800


class Prog:
    def __init__(self, which, handle, sems):
        self.which = which
        self.handle = handle
        self.sems = sems
        self.c = {k: 0 for k in sems}

    def do(self, eng, fn):
        if self.which == eng:
            return fn(self.handle)

    CHAIN = {"ve": "vm", "sc": "sm"}

    def inc(self, eng, fn, sem, n=1):
        ch = self.CHAIN.get(eng)
        if ch is not None and self.which == eng:
            self.handle.wait_ge(self.sems[ch], self.c[ch])
        self.c[sem] += n
        if self.which == eng:
            fn(self.handle).then_inc(self.sems[sem], n)
        return self.c[sem]

    def dma(self, eng, fn, stream):
        tot = self.c[stream]
        if self.which == eng and tot > 0:
            self.handle.wait_ge(self.sems[stream], tot)
        self.c[stream] += 16
        if self.which == eng:
            fn(self.handle).then_inc(self.sems[stream], 16)
        return (stream, self.c[stream])

    def waitv(self, eng, sv):
        self.wait(eng, sv[0], sv[1])

    def inc_default(self, eng, fn, sem):
        self.c[sem] += 1
        if self.which == eng:
            fn(self.handle).then_inc(self.sems[sem])
        return self.c[sem]

    def wait(self, eng, sem, val):
        assert val <= self.c[sem], (eng, sem, val, self.c[sem])
        if self.which == eng:
            self.handle.wait_ge(self.sems[sem], val)


def build_nc():
    nc = bass.Bass("TRN2", target_bir_lowering=False, debug=False)

    x_in = nc.dram_tensor("x", [TC, C], F32, kind="ExternalInput").ap()
    xT_in = nc.dram_tensor("xT", [C, TC], F32, kind="ExternalInput").ap()
    wqT = nc.dram_tensor("wqT", [C, C], F32, kind="ExternalInput").ap()
    wkT = nc.dram_tensor("wkT", [C, C], F32, kind="ExternalInput").ap()
    wvT = nc.dram_tensor("wvT", [C, C], F32, kind="ExternalInput").ap()
    woT = nc.dram_tensor("woT", [C, C], F32, kind="ExternalInput").ap()
    w1T = nc.dram_tensor("w1T", [C, FF], F32, kind="ExternalInput").ap()
    w2T = nc.dram_tensor("w2T", [FF, C], F32, kind="ExternalInput").ap()
    b1c_in = nc.dram_tensor("b1c", [128, NFB], F32, kind="ExternalInput").ap()
    b2r_in = nc.dram_tensor("b2r", [1, C], F32, kind="ExternalInput").ap()
    out_t = nc.dram_tensor("out", [TC, C], F32, kind="ExternalOutput").ap()

    x2s = nc.dram_tensor("x2s", [TC, C], BF16).ap()
    hts = nc.dram_tensor("hts", [FF, TC], BF16).ap()
    w2qs = nc.dram_tensor("w2qs", [FF, C], BF16).ap()
    pre0s = nc.dram_tensor("pre0s", [TC, DHALF], F32).ap()
    cc_in1 = nc.dram_tensor("cc_in1", [128, KV_COLS // 2], F32)
    cc_out1 = nc.dram_tensor("cc_out1", [128, KV_COLS // 2], F32, addr_space="Shared")
    cc_in2 = nc.dram_tensor("cc_in2", [128, KV_COLS // 2], F32)
    cc_out2 = nc.dram_tensor("cc_out2", [128, KV_COLS // 2], F32, addr_space="Shared")

    with contextlib.ExitStack() as es:
        E = es.enter_context
        arena = E(nc.sbuf_tensor("arena", [128, ARENA_BYTES], U8))
        ar = arena.ap()

        def V(off, cols, dt, rows=128, tag=""):
            nbytes = cols * (4 if dt == F32 else 2)
            assert off % 32 == 0 and off + nbytes <= ARENA_BYTES, (tag, off, nbytes)
            return ar[0:rows, off:off + nbytes].bitcast(dt)

        co = [C_OFF]
        def CV(cols, dt, rows=128):
            v = V(co[0], cols, dt, rows)
            co[0] += ((cols * (4 if dt == F32 else 2) + 63) // 64) * 64
            return v
        ones_row_f = CV(128, F32, rows=1)
        ones_row_b = CV(128, BF16, rows=1)
        ones_col_f = CV(1, F32)
        eps_ln = CV(1, F32)
        thr1p, thr1n = CV(1, F32), CV(1, F32)
        thr2p, thr2n = CV(1, F32), CV(1, F32)
        c_b1 = CV(NFB, F32)
        wsum = CV(64, F32)
        c_red = CV(1, F32)
        thr_tmp = CV(4, F32, rows=1)
        eps_z6 = CV(1, F32, rows=1)
        ln_s1 = [CV(1, F32), CV(1, F32)]
        ln_s2 = [CV(1, F32), CV(1, F32)]
        ln_mean = [CV(1, F32), CV(1, F32)]
        ln_var = [CV(1, F32), CV(1, F32)]
        ln_sd = [CV(1, F32), CV(1, F32)]
        ln_rstd = [CV(1, F32), CV(1, F32)]
        ln_nm = [CV(1, F32), CV(1, F32)]
        s1a = CV(NT, F32)
        s2a = CV(NT, F32)
        assert co[0] <= ARENA_BYTES, co

        A_xT = V(A_OFF, CT * TC, BF16, tag="A")
        A_woT = A_xT
        A_x2T = A_xT
        A_w2hi = A_xT
        B_wkT = V(B_OFF, CT * C, BF16, tag="B")
        B_yT = B_wkT
        B_w1q = [V(B_OFF + i * 16384, CT * 512, BF16, tag="Bw1") for i in range(4)]
        B_w2lo = B_wkT
        S_wv = [V(S_OFF + i * 16384, CT * 512, BF16, tag="Swv") for i in range(2)]
        S_wq = S_wv
        S_xres = [V(S_OFF + i * 8192, C, F32, tag="Sxr") for i in range(2)]
        S_x2bf = [V(S_OFF + 16384 + i * 4096, C, BF16, tag="Sx2") for i in range(2)]
        S_hstage = [V(S_OFF + i * 4096, C, BF16, tag="Shs") for i in range(2)]
        S_ht2 = [V(S_OFF + i * 16384, 32 * 256, BF16, tag="Sht2") for i in range(2)]
        T_x2res = [V(T_OFF + i * 2048, DHALF, BF16, tag="Tx2r") for i in range(2)]
        T_ost = [V(T_OFF + 4096 + i * 2048, 512, F32, tag="Tost") for i in range(2)]
        Q_p1stg = [V(Q_OFF + i * 4096, C, BF16, tag="Qp1") for i in range(2)]
        Q_absd = V(Q_OFF + 8192, C, BF16, tag="Qab")
        Q_fmin = [V(Q_OFF + 12288 + i * 2048, 512, F32, tag="Qfm") for i in range(2)]
        Q_fexp = [V(Q_OFF + 16384 + i * 2048, 512, F32, tag="Qfe") for i in range(2)]
        Q_qT = [V(Q_OFF + 20480 + i * 4096, TC, BF16, tag="QqT") for i in range(2)]
        Q_zb = [V(Q_OFF + 28672 + i * 2048, 512, F32, tag="Qzb") for i in range(2)]
        kv_loc = V(Q_OFF + 20480, KV_COLS, F32, tag="kvloc")
        Q_pre = [V(Q_OFF + 16384 + i * 8192, C, F32, tag="Qpre") for i in range(2)]
        Q_q2stg = [V(Q_OFF + i * 2048, 512, F32, tag="Qq2") for i in range(4)]
        Q_sgnA = V(Q_OFF + 8192, 1024, BF16, tag="QsA")
        Q_sgnB = V(Q_OFF + 10240, 1024, BF16, tag="QsB")
        Q_w2stg = [V(Q_OFF + 12288 + i * 4096, 1024, F32, tag="Qw2s") for i in range(2)]
        Q_sgn2A = V(Q_OFF + 20480, 1024, BF16, tag="Qs2A")
        Q_sgn2B = V(Q_OFF + 22528, 1024, BF16, tag="Qs2B")
        Q_w2out = [V(Q_OFF + 24576 + i * 2048, 1024, BF16, tag="Qw2o") for i in range(2)]
        Q_vg = [V(Q_OFF + 28672 + i * 2048, 1024, BF16, tag="Qvg") for i in range(2)]
        Q_m2pre = [V(Q_OFF + i * 4096, DHALF, F32, tag="Qm2p") for i in range(2)]
        Q_r2 = [V(Q_OFF + 8192 + i * 4096, DHALF, F32, tag="Qr2") for i in range(2)]
        Q_b2b = V(Q_OFF + 16384, DHALF, F32, tag="Qb2b")
        Q_p0rd = [V(Q_OFF + 20480 + i * 4096, DHALF, F32, tag="Qp0") for i in range(2)]
        Q_b2row = V(Q_OFF + 28672, DHALF, F32, rows=1, tag="Qb2r")
        T_kbf = [V(T_OFF + i * 1024, 512, BF16, tag="Tk") for i in range(2)]
        T_vaug = [V(T_OFF + 2048 + i * 1280, 4 * VSLOT, BF16, tag="Tv") for i in range(2)]
        T_kvbf = V(T_OFF + 4608, KV_COLS, BF16, tag="Tkv")
        T_sq = V(T_OFF + 8832, C, BF16, tag="Tsq")
        z_ln = V(T_OFF + 8832, 512, F32, rows=1, tag="zln")
        z_bf = [V(T_OFF + i * 1024, 512, BF16, rows=1, tag="zbf")
                for i in range(4)]
        assert T_OFF + 8832 + 2 * C <= C_OFF

        psA = E(nc.psum_tensor("psA", [128, 2048], F32))
        psB = E(nc.psum_tensor("psB", [128, 2048], F32))

        sems = {}
        stream_names = (
            ["tm", "vm", "sm", "cc", "mi"]
            + [f"xt{i}" for i in range(4)] + [f"wk{i}" for i in range(4)]
            + ["wv0", "wv1", "q1a", "q1b", "wo0", "wo1", "mig",
               "x2T0", "x2T1", "x2T2", "x2T3",
               "xr0", "xr1", "x20", "x21", "p2a", "p2b", "p3a", "p3b",
               "p4a", "p4b",
               "wqs0", "wqs1", "ht0", "ht1", "wh0", "wh1",
               "hs0", "hs1", "hs2", "x2r0", "x2r1",
               "p00", "p01", "p0r0", "p0r1", "so0", "so1"])
        for name in stream_names:
            sems[name] = E(nc.semaphore(name))
        block = E(nc.Block())

        def prog(P: Prog):
            P.inc("ve", lambda e: e.memset(ones_row_f, 1.0), "vm")
            P.inc("ve", lambda e: e.memset(ones_row_b, 1.0), "vm")
            P.inc("ve", lambda e: e.memset(ones_col_f, 1.0), "vm")
            P.inc("ve", lambda e: e.memset(eps_ln, 1e-5), "vm")
            P.inc("ve", lambda e: e.memset(eps_z6, 1e-6), "vm")
            P.inc("ve", lambda e: e.memset(T_vaug[0][:, :], 0.0), "vm")
            P.inc("ve", lambda e: e.memset(T_vaug[1][:, :], 0.0), "vm")
            for par in range(2):
                for hh in range(4):
                    P.inc("ve", lambda e, par=par, hh=hh: e.memset(
                        T_vaug[par][:, hh * VSLOT + 128: hh * VSLOT + 129], 1.0), "vm")
            vm_setup_done = P.c["vm"]

            sv_b1 = P.dma("sy", lambda e: e.dma_start(c_b1, b1c_in[:, :]), "mi")

            sv_xt = [None] * CT
            sv_wk = [None] * CT
            def _wslot_load(e, src_t, g, par):
                dst = S_wv[par].rearrange("p (ct o) -> p ct o", ct=CT)
                src = src_t[:, g * 512:(g + 1) * 512].rearrange(
                    "(ct p) o -> p ct o", p=128)
                return e.dma_start(dst, src)
            sv_wv0 = P.dma("gp", lambda e: _wslot_load(e, wvT, 0, 0), "wv0")
            for ct in range(CT):
                sv_xt[ct] = P.dma("gp", lambda e, ct=ct: e.dma_start(
                    A_xT[:, ct * TC:(ct + 1) * TC],
                    xT_in[ct * 128:(ct + 1) * 128, :]), f"xt{ct % 4}")
                sv_wk[ct] = P.dma("gp", lambda e, ct=ct: e.dma_start(
                    B_wkT[:, ct * C:(ct + 1) * C],
                    wkT[ct * 128:(ct + 1) * 128, :]), f"wk{ct % 4}")
            sv_wv = [None] * 4
            sv_wv[0] = sv_wv0
            sv_wv[1] = P.dma("gp", lambda e: _wslot_load(e, wvT, 1, 1), "wv1")

            vm_kv_evac = [0] * 4
            sm_vaug = {}
            tm_kmm = {}
            tm_kvmm = {}
            vm_fm = {}
            sm_p1chunk = {}

            def p1_pass1_chunk(i, src_t):
                rt, fc = divmod(i, 4)
                par = i % 2
                if i >= 2:
                    P.wait("gp", *sm_p1chunk[i - 2])
                sv_here = P.dma("gp", lambda e: e.dma_start(
                    Q_p1stg[par],
                    src_t[rt * 128:(rt + 1) * 128, fc * 2048:(fc + 1) * 2048]),
                    "q1a" if par == 0 else "q1b")
                if par == 0:
                    P.waitv("sc", sv_here)
                    v = P.inc("sc", lambda e, i=i: e.activation(
                        Q_absd, Q_p1stg[0], AF.Abs, accum_out=wsum[:, i:i + 1]), "sm")
                    sm_p1chunk[i] = ("sm", v)
                else:
                    P.waitv("ve", sv_here)
                    v = P.inc("ve", lambda e, i=i: e.tensor_reduce(
                        wsum[:, i:i + 1], Q_p1stg[1], AX, ALU.add,
                        apply_absolute_value=True), "vm")
                    sm_p1chunk[i] = ("vm", v)

            sv_wq0_early = [None]

            def _emit_kv_acc(g, tt):
                kpar = tt % 2
                P.wait("te", "vm", vm_fm[(g, tt)])
                P.wait("te", "sm", sm_vaug[(g, tt)])
                if tt == 0 and g > 0:
                    P.wait("te", "vm", vm_kv_evac[g - 1])
                def _kv_acc(t, tt=tt, kpar=kpar):
                    for hh in range(4):
                        mm = t.matmul(
                            psB[:, hh * 512: hh * 512 + 132],
                            T_kbf[kpar][:, hh * 128:(hh + 1) * 128],
                            T_vaug[kpar][:, hh * VSLOT: hh * VSLOT + 132],
                            start=(tt == 0), stop=(tt == NT - 1))
                    return mm
                tm_kvmm[(g, tt)] = P.inc("te", _kv_acc, "tm")

            for g in range(4):
                for tt in range(NT):
                    kpar = tt % 2
                    k_sl = psA[:, kpar * 512:(kpar + 1) * 512]
                    v_sl = psA[:, 1024 + kpar * 512: 1024 + (kpar + 1) * 512]
                    if g > 0 or tt > 0:
                        P.waitv("te", sv_wv[g])
                    prev = (g, tt - 2) if tt >= 2 else (g - 1, NT - 2 + tt)
                    if prev in vm_fm:
                        P.wait("te", "vm", vm_fm[prev])
                        P.wait("te", "sm", sm_vaug[prev])
                    if g == 0 and tt == 0:
                        P.wait("te", "vm", vm_setup_done)
                        P.waitv("te", sv_wv[0])
                        for ct in range(CT):
                            P.waitv("te", sv_xt[ct])
                            P.waitv("te", sv_wk[ct])
                            def _kv_ct(t, ct=ct):
                                lhs = A_xT[:, ct * TC: ct * TC + 128]
                                t.matmul(k_sl, lhs,
                                         B_wkT[:, ct * C: ct * C + 512],
                                         start=(ct == 0), stop=(ct == CT - 1))
                                return t.matmul(v_sl, lhs,
                                         S_wv[0][:, ct * 512:(ct + 1) * 512],
                                         start=(ct == 0), stop=(ct == CT - 1))
                            tm_here = P.inc("te", _kv_ct, "tm")
                    else:
                        def _kv_proj(t, g=g, tt=tt, k_sl=k_sl, v_sl=v_sl):
                            for ct in range(CT):
                                lhs = A_xT[:, ct * TC + tt * 128:
                                           ct * TC + tt * 128 + 128]
                                t.matmul(k_sl, lhs,
                                         B_wkT[:, ct * C + g * 512:
                                               ct * C + g * 512 + 512],
                                         start=(ct == 0), stop=(ct == CT - 1))
                                mm = t.matmul(v_sl, lhs,
                                         S_wv[g % 2][:, ct * 512:(ct + 1) * 512],
                                         start=(ct == 0), stop=(ct == CT - 1))
                            return mm
                        tm_here = P.inc("te", _kv_proj, "tm")
                    tm_kmm[(g, tt)] = tm_here

                    if tt > 0:
                        _emit_kv_acc(g, tt - 1)

                    P.wait("ve", "tm", tm_here)
                    if prev in tm_kvmm:
                        P.wait("ve", "tm", tm_kvmm[prev])
                    vmin = P.inc("ve", lambda e, k_sl=k_sl, kpar=kpar: e.tensor_scalar(
                        Q_fmin[kpar], k_sl, 0.0, None, ALU.min), "vm")
                    P.wait("sc", "vm", vmin)
                    sexp = P.inc("sc", lambda e, kpar=kpar: e.activation(
                        Q_fexp[kpar], Q_fmin[kpar], AF.Exp), "sm")
                    P.wait("ve", "sm", sexp)
                    vm_fm[(g, tt)] = P.inc("ve", lambda e, k_sl=k_sl, kpar=kpar:
                        e.scalar_tensor_tensor(
                            T_kbf[kpar][:, :], k_sl, 0.0, Q_fexp[kpar],
                            ALU.max, ALU.add), "vm")
                    P.wait("sc", "tm", tm_here)
                    if prev in tm_kvmm:
                        P.wait("sc", "tm", tm_kvmm[prev])
                    def _vcopy(e, v_sl=v_sl, kpar=kpar):
                        dst = T_vaug[kpar].rearrange(
                            "p (s c) -> p s c", s=4)[:, :, 0:128]
                        src = v_sl.rearrange("p (s c) -> p s c", s=4)
                        return e.copy(dst, src)
                    sm_vaug[(g, tt)] = P.inc("sc", _vcopy, "sm")

                    i = g * NT + tt
                    if i < 64:
                        p1_pass1_chunk(i, w1T)

                _emit_kv_acc(g, NT - 1)
                if g + 2 < 4:
                    P.wait("gp", "tm", tm_kmm[(g, NT - 1)])
                    sv_wv[g + 2] = P.dma(
                        "gp", lambda e, g=g: _wslot_load(e, wvT, g + 2, g % 2),
                        f"wv{g % 2}")
                if g == 2:
                    P.wait("gp", "tm", tm_kmm[(2, NT - 1)])
                    sv_wq0_early[0] = P.dma(
                        "gp", lambda e: _wslot_load(e, wqT, 0, 0), "wv0")
                P.wait("ve", "tm", tm_kvmm[(g, NT - 1)])
                def _kv_evac(v, g=g):
                    for hh in range(4):
                        hid = g * 4 + hh
                        r = v.tensor_copy(
                            kv_loc[:, hid * KSLOT:(hid + 1) * KSLOT],
                            psB[:, hh * 512: hh * 512 + KSLOT])
                    return r
                vm_kv_evac[g] = P.inc("ve", _kv_evac, "vm")

            P.wait("ve", *sm_p1chunk[62])
            P.wait("ve", *sm_p1chunk[63])
            vm_red1 = P.inc("ve", lambda e: e.tensor_reduce(
                c_red, wsum[:, :], AX, ALU.add), "vm")
            P.wait("te", "vm", vm_red1)
            P.wait("te", "vm", vm_kv_evac[3])
            tm_t1a = P.inc("te", lambda e: e.matmul(
                psB[0:1, 0:1], c_red, ones_col_f, start=True, stop=True), "tm")
            P.wait("ve", "tm", tm_t1a)
            vm_t1b = P.inc("ve", lambda e: e.tensor_scalar(
                thr_tmp[0:1, 0:1], psB[0:1, 0:1], 0.7 / (FF * C), None, ALU.mult), "vm")
            P.wait("te", "vm", vm_t1b)
            tm_t1b = P.inc("te", lambda e: e.matmul(
                psB[:, 512:513], ones_row_f, thr_tmp[0:1, 0:1],
                start=True, stop=True), "tm")
            P.wait("ve", "tm", tm_t1b)
            P.inc("ve", lambda e: e.tensor_copy(thr1p, psB[:, 512:513]), "vm")
            vm_thr1 = P.inc("ve", lambda e: e.tensor_scalar(
                thr1n, psB[:, 512:513], -1.0, None, ALU.mult), "vm")

            HCOL = 8 * KSLOT
            sv_wq = [None] * 4
            P.wait("gp", "vm", vm_kv_evac[1])
            sv_ccin1 = P.dma("gp", lambda e: e.dma_start(
                cc_in1[:, :], kv_loc[:, 0:HCOL]), "mig")
            P.waitv("gp", sv_ccin1)
            cc1 = P.inc_default("gp", lambda e: e.collective_compute(
                "AllReduce", ALU.add, replica_groups=[list(range(NCORES))],
                ins=[cc_in1.ap().opt()], outs=[cc_out1.ap().opt()]), "cc")
            sv_wq[0] = sv_wq0_early[0]
            P.wait("gp", "cc", cc1)
            sv_kvbf1 = P.dma("gp", lambda e: e.dma_start(
                T_kvbf[:, 0:HCOL], cc_out1[:, :]), "mig")
            P.wait("gp", "vm", vm_kv_evac[3])
            sv_ccin = P.dma("gp", lambda e: e.dma_start(
                cc_in2[:, :], kv_loc[:, HCOL:KV_COLS]), "mig")
            P.waitv("gp", sv_ccin)
            cc2 = P.inc_default("gp", lambda e: e.collective_compute(
                "AllReduce", ALU.add, replica_groups=[list(range(NCORES))],
                ins=[cc_in2.ap().opt()], outs=[cc_out2.ap().opt()]), "cc")
            P.wait("gp", "tm", tm_kmm[(3, NT - 1)])
            sv_wq[1] = P.dma("gp", lambda e: _wslot_load(e, wqT, 1, 1), "wv1")
            P.wait("gp", "cc", cc2)
            sv_kvbf = P.dma("gp", lambda e: e.dma_start(
                T_kvbf[:, HCOL:KV_COLS], cc_out2[:, :]), "mig")


            tm_q01 = [0] * H
            tm_q23 = [0] * H
            vm_fm01 = [0] * H
            vm_fm23 = [0] * H
            tm_y, tm_den, tm_zb = {}, {}, {}
            sm_zln, sm_zexp = {}, {}
            vm_zbev, vm_yev = {}, {}
            sm_p2chunk = {}
            p2i = [0]

            p2_pending = {}

            def w2_chunk_issue():
                i = p2i[0]
                if i >= 64:
                    return
                p2i[0] += 1
                par = i % 2
                if i >= 2:
                    w2_chunk_consume(i - 2)
                    P.wait("gp", *sm_p2chunk[i - 2])
                else:
                    P.wait("gp", *sm_p1chunk[62 + i])
                p2_pending[i] = P.dma("gp", lambda e, i=i: e.dma_start(
                    Q_p1stg[par], w2T[i * 128:(i + 1) * 128, :]),
                    "q1a" if par == 0 else "q1b")

            def w2_chunk_consume(i):
                if i not in p2_pending or i in sm_p2chunk:
                    return
                par = i % 2
                if par == 0:
                    P.waitv("sc", p2_pending[i])
                    v = P.inc("sc", lambda e, i=i: e.activation(
                        Q_absd, Q_p1stg[0], AF.Abs, accum_out=wsum[:, i:i + 1]), "sm")
                    sm_p2chunk[i] = ("sm", v)
                else:
                    P.waitv("ve", p2_pending[i])
                    v = P.inc("ve", lambda e, i=i: e.tensor_reduce(
                        wsum[:, i:i + 1], Q_p1stg[1], AX, ALU.add,
                        apply_absolute_value=True), "vm")
                    sm_p2chunk[i] = ("vm", v)

            def _fm_pair(h, ts0):
                qpar = h % 2
                qo = (h % 2) * 1024
                out = 0
                for ts in (ts0, ts0 + 1):
                    fpar = ts % 2
                    vmin = P.inc("ve", lambda e, ts=ts, fpar=fpar: e.tensor_scalar(
                        Q_fmin[fpar],
                        psA[:, qo + (ts % 2) * 512: qo + ((ts % 2) + 1) * 512],
                        0.0, None, ALU.min), "vm")
                    P.wait("sc", "vm", vmin)
                    sexp = P.inc("sc", lambda e, fpar=fpar: e.activation(
                        Q_fexp[fpar], Q_fmin[fpar], AF.Exp), "sm")
                    P.wait("ve", "sm", sexp)
                    out = P.inc("ve", lambda e, ts=ts, fpar=fpar, qpar=qpar:
                        e.scalar_tensor_tensor(
                            Q_qT[qpar][:, ts * 512:(ts + 1) * 512],
                            psA[:, qo + (ts % 2) * 512: qo + ((ts % 2) + 1) * 512],
                            0.0, Q_fexp[fpar], ALU.max, ALU.add), "vm")
                return out

            def _q_half(h, ts0):
                g4, hh = divmod(h, 4)
                qo = (h % 2) * 1024
                def _q(t, g4=g4, hh=hh, ts0=ts0):
                    for ct in range(CT):
                        lhs = S_wq[g4 % 2][:, ct * 512 + hh * 128:
                                           ct * 512 + hh * 128 + 128]
                        for j, ts in enumerate((ts0, ts0 + 1)):
                            mm = t.matmul(
                                psA[:, qo + j * 512: qo + (j + 1) * 512], lhs,
                                A_xT[:, ct * TC + ts * 512: ct * TC + (ts + 1) * 512],
                                start=(ct == 0), stop=(ct == CT - 1))
                    return mm
                return P.inc("te", _q, "tm")

            for step in range(H + 1):
                h = step
                hp = step - 1
                if h < H:
                    g4, hh = divmod(h, 4)
                    P.waitv("te", sv_wq[g4])
                    if h == 0:
                        P.wait("te", "vm", vm_fm[(3, NT - 1)])
                        P.wait("te", "vm", vm_thr1)
                    elif h == 1:
                        P.wait("te", "sm", sm_vaug[(3, NT - 1)])
                    else:
                        P.wait("te", "vm", vm_zbev[(h - 2, 3)])
                    tm_q01[h] = _q_half(h, 0)

                if hp >= 0:
                    qpar = hp % 2
                    if hp == 0:
                        P.waitv("te", sv_kvbf1)
                    if hp == 8:
                        P.waitv("te", sv_kvbf)
                    P.wait("te", "vm", vm_fm23[hp])
                    for ts in range(4):
                        sl = ts % 2
                        prev = (hp, ts - 2) if ts >= 2 else (hp - 1, 2 + ts)
                        if prev in vm_yev:
                            P.wait("te", "vm", vm_yev[prev])
                        tm_y[(hp, ts)] = P.inc("te", lambda e, hp=hp, ts=ts, sl=sl,
                                               qpar=qpar: e.matmul(
                            psB[:, sl * 512:(sl + 1) * 512],
                            T_kvbf[:, hp * KSLOT: hp * KSLOT + 128],
                            Q_qT[qpar][:, ts * 512:(ts + 1) * 512],
                            start=True, stop=True), "tm")
                        if prev in sm_zln:
                            P.wait("te", "sm", sm_zln[prev])
                        tm_den[(hp, ts)] = P.inc("te", lambda e, hp=hp, ts=ts, sl=sl,
                                                 qpar=qpar: e.matmul(
                            psB[0:1, 1024 + sl * 512: 1024 + (sl + 1) * 512],
                            T_kvbf[:, hp * KSLOT + 128: hp * KSLOT + 129],
                            Q_qT[qpar][:, ts * 512:(ts + 1) * 512],
                            start=True, stop=True), "tm")
                        P.wait("sc", "tm", tm_den[(hp, ts)])
                        if (hp - 1, ts) in tm_zb:
                            P.wait("sc", "tm", tm_zb[(hp - 1, ts)])
                        sm_zln[(hp, ts)] = P.inc("sc", lambda e, sl=sl: e.activation(
                            z_ln, psB[0:1, 1024 + sl * 512: 1024 + (sl + 1) * 512],
                            AF.Ln, bias=eps_z6), "sm")
                        sm_zexp[(hp, ts)] = P.inc("sc", lambda e, ts=ts: e.activation(
                            z_bf[ts], z_ln, AF.Exp, scale=-1.0), "sm")

                def _zb_part(hp, tss):
                    zo = (hp % 2) * 1024
                    for ts in tss:
                        sl = ts % 2
                        prev = (hp, ts - 2) if ts >= 2 else (hp - 1, 2 + ts)
                        if prev in vm_zbev:
                            P.wait("te", "vm", vm_zbev[prev])
                        P.wait("te", "sm", sm_zexp[(hp, ts)])
                        tm_zb[(hp, ts)] = P.inc("te", lambda e, sl=sl, ts=ts, zo=zo:
                            e.matmul(psA[:, zo + sl * 512: zo + (sl + 1) * 512],
                                     ones_row_b, z_bf[ts],
                                     start=True, stop=True), "tm")
                    for ts in tss:
                        sl = ts % 2
                        P.wait("ve", "tm", tm_zb[(hp, ts)])
                        vm_zbev[(hp, ts)] = P.inc("ve", lambda e, sl=sl, zo=zo:
                            e.tensor_copy(
                                Q_zb[sl], psA[:, zo + sl * 512: zo + (sl + 1) * 512]),
                            "vm")
                    for ts in tss:
                        sl = ts % 2
                        P.wait("ve", "tm", tm_y[(hp, ts)])
                        vm_yev[(hp, ts)] = P.inc("ve", lambda e, hp=hp, ts=ts, sl=sl:
                            e.tensor_tensor(
                                B_yT[:, hp * TC + ts * 512: hp * TC + (ts + 1) * 512],
                                psB[:, sl * 512:(sl + 1) * 512],
                                Q_zb[sl], ALU.mult), "vm")

                if hp >= 0:
                    _zb_part(hp, (0, 1))

                if h < H:
                    P.wait("ve", "tm", tm_q01[h])
                    if h == 0:
                        P.waitv("ve", sv_ccin1)
                        P.waitv("ve", sv_ccin)
                    if h >= 2:
                        P.wait("ve", "tm", tm_den[(h - 2, 3)])
                    vm_fm01[h] = _fm_pair(h, 0)

                if hp >= 0:
                    _zb_part(hp, (2, 3))

                if h < H:
                    P.wait("te", "vm", vm_fm01[h])
                    tm_q23[h] = _q_half(h, 2)
                    P.wait("ve", "tm", tm_q23[h])
                    vm_fm23[h] = _fm_pair(h, 2)

                    g4, hh = divmod(h, 4)
                    if hh == 3 and g4 + 2 <= 3:
                        P.wait("gp", "tm", tm_q23[h])
                        sv_wq[g4 + 2] = P.dma(
                            "gp", lambda e, g4=g4: _wslot_load(e, wqT, g4 + 2, g4 % 2),
                            f"wv{g4 % 2}")
                    w2_chunk_issue()
                    w2_chunk_issue()

            tm_q = tm_q23
            vm_p2 = P.c["vm"]

            P.wait("gp", "tm", tm_q[H - 1])
            sv_wo = []
            for ct in range(CT):
                sv_wo.append(P.dma("gp", lambda e, ct=ct: e.dma_start(
                    A_woT[:, ct * TC:(ct + 1) * TC],
                    woT[ct * 128:(ct + 1) * 128, :]), f"wo{ct % 2}"))

            sv_xres = {}
            P.wait("sy", "tm", tm_q[H - 1])
            for tt in range(2):
                sv_xres[tt] = P.dma("sy", lambda e, tt=tt: e.dma_start(
                    S_xres[tt % 2], x_in[tt * 128:(tt + 1) * 128, :]), f"xr{tt % 2}")
            tm_p3 = {}
            vm_p3pre = {}
            sm_x2 = {}
            sv_x2 = {}
            for tt in range(NT):
                par = tt % 2
                ps = psA if par == 0 else psB
                if tt >= 2:
                    P.wait("te", "vm", vm_p3pre[tt - 2])
                if tt == 0:
                    P.wait("te", "vm", vm_p2)
                    for ct in range(CT):
                        P.waitv("te", sv_wo[ct])
                        def _p3c(t, ct=ct, ps=ps):
                            lhs = B_yT[:, ct * TC: ct * TC + 128]
                            for ds in range(4):
                                mm = t.matmul(
                                    ps[:, ds * 512:(ds + 1) * 512], lhs,
                                    A_woT[:, ct * TC + ds * 512:
                                          ct * TC + (ds + 1) * 512],
                                    start=(ct == 0), stop=(ct == CT - 1))
                            return mm
                        tm_p3[tt] = P.inc("te", _p3c, "tm")
                else:
                    def _p3_mms(t, tt=tt, ps=ps):
                        for ct in range(CT):
                            lhs = B_yT[:, ct * TC + tt * 128: ct * TC + tt * 128 + 128]
                            for ds in range(4):
                                mm = t.matmul(
                                    ps[:, ds * 512:(ds + 1) * 512], lhs,
                                    A_woT[:, ct * TC + ds * 512:
                                          ct * TC + (ds + 1) * 512],
                                    start=(ct == 0), stop=(ct == CT - 1))
                        return mm
                    tm_p3[tt] = P.inc("te", _p3_mms, "tm")

                P.wait("ve", "tm", tm_p3[tt])
                P.waitv("ve", sv_xres[tt])
                if tt >= 2:
                    P.wait("ve", "sm", sm_x2[tt - 2])
                vm_pre = P.inc("ve", lambda e, ps=ps, par=par: e.scalar_tensor_tensor(
                    Q_pre[par], ps[:, :], 0.0, S_xres[par],
                    ALU.bypass, ALU.add, accum_out=ln_s1[par]), "vm")
                vm_p3pre[tt] = vm_pre
                if tt + 2 < NT:
                    P.wait("sy", "vm", vm_pre)
                    sv_xres[tt + 2] = P.dma("sy", lambda e, tt=tt, par=par: e.dma_start(
                        S_xres[par], x_in[(tt + 2) * 128:(tt + 3) * 128, :]),
                        f"xr{par}")
                P.wait("sc", "vm", vm_pre)
                sm_sq3 = P.inc("sc", lambda e, par=par: e.activation(
                    T_sq, Q_pre[par], AF.Square, accum_out=ln_s2[par]), "sm")
                P.wait("ve", "sm", sm_sq3)
                P.inc("ve", lambda e, par=par: e.tensor_scalar(
                    ln_mean[par], ln_s1[par], 1.0 / C, None, ALU.mult), "vm")
                P.inc("ve", lambda e, par=par: e.tensor_scalar(
                    ln_var[par], ln_s2[par], 1.0 / C, None, ALU.mult), "vm")
                P.inc("ve", lambda e, par=par: e.scalar_tensor_tensor(
                    ln_nm[par], ln_mean[par], -1.0, ln_mean[par],
                    ALU.mult, ALU.mult), "vm")
                vm_var = P.inc("ve", lambda e, par=par: e.tensor_tensor(
                    ln_var[par], ln_var[par], ln_nm[par], ALU.add), "vm")
                P.wait("sc", "vm", vm_var)
                sm_sd = P.inc("sc", lambda e, par=par: e.activation(
                    ln_sd[par], ln_var[par], AF.Sqrt, bias=eps_ln), "sm")
                P.wait("ve", "sm", sm_sd)
                P.inc("ve", lambda e, par=par: e.reciprocal(
                    ln_rstd[par], ln_sd[par]), "vm")
                vm_coef = P.inc("ve", lambda e, par=par: e.scalar_tensor_tensor(
                    ln_nm[par], ln_mean[par], -1.0, ln_rstd[par],
                    ALU.mult, ALU.mult), "vm")
                P.wait("sc", "vm", vm_coef)
                if tt >= 2:
                    P.waitv("sc", sv_x2[tt - 2])
                sm_x2[tt] = P.inc("sc", lambda e, par=par: e.activation(
                    S_x2bf[par], Q_pre[par], AF.Identity,
                    bias=ln_nm[par], scale=ln_rstd[par]), "sm")
                P.wait("gp", "sm", sm_x2[tt])
                sv_x2[tt] = P.dma("gp", lambda e, tt=tt, par=par: e.dma_start(
                    x2s[tt * 128:(tt + 1) * 128, :], S_x2bf[par]), f"x2{par}")

                w2_chunk_issue()
                w2_chunk_issue()
                w2_chunk_issue()
            while p2i[0] < 64:
                w2_chunk_issue()
                w2_chunk_issue()
            for i in range(64):
                w2_chunk_consume(i)

            sm_w1sg = {}
            vm_w1add = {}
            vm_w1q = [0] * 16
            tm_mm1 = {}
            sm_gelu = {}
            sv_ht = {}
            sm_w2sg = {}
            vm_w2add = {}
            sv_w2q = {}
            w1q_sv = {}

            def w1q_issue(qgrp, ct):
                i = qgrp * CT + ct
                b = i % 4
                if i >= 4:
                    P.wait("sy", *sm_w1sg[i - 4])
                else:
                    P.wait("sy", *sm_p2chunk[62 + i // 2])
                w1q_sv[i] = P.dma("sy", lambda e, qgrp=qgrp, ct=ct, b=b: e.dma_start(
                    Q_q2stg[b],
                    w1T[ct * 128:(ct + 1) * 128,
                        qgrp * 512:(qgrp + 1) * 512]),
                    ("p2a", "p2b", "p4a", "p4b")[b])

            def w1q_consume(qgrp, ct):
                i = qgrp * CT + ct
                b = i % 4
                par = i % 2
                src = Q_q2stg[b]
                if ct == 0:
                    if qgrp >= 4:
                        P.wait("ve", "tm", tm_mm1[(qgrp - 4) * 4 + 3])
                    else:
                        P.wait("ve", "tm", tm_p3[NT - 1])
                dstB = B_w1q[qgrp % 4][:, ct * 512:(ct + 1) * 512]
                if par == 0:
                    P.waitv("sc", w1q_sv[i])
                    if i == 0:
                        P.wait("sc", "vm", vm_thr1)
                    if i >= 2:
                        P.wait("sc", "vm", vm_w1add[i - 2])
                    P.inc("sc", lambda e, src=src: e.activation(
                        Q_sgnA[:, 0:512], src, AF.Sign, bias=thr1n), "sm")
                    s = P.inc("sc", lambda e, src=src: e.activation(
                        Q_sgnB[:, 0:512], src, AF.Sign, bias=thr1p), "sm")
                    sm_w1sg[i] = ("sm", s)
                    P.wait("ve", "sm", s)
                    vm_w1add[i] = P.inc("ve", lambda e, dstB=dstB: e.tensor_tensor(
                        dstB, Q_sgnA[:, 0:512], Q_sgnB[:, 0:512], ALU.add), "vm")
                else:
                    P.waitv("ve", w1q_sv[i])
                    if i == 1:
                        P.wait("ve", "vm", vm_thr1)
                    P.inc("ve", lambda e, src=src: e.tensor_scalar(
                        Q_vg[0][:, 0:512], src,
                        thr1p, 2.0, ALU.is_ge, ALU.mult), "vm")
                    P.inc("ve", lambda e, src=src: e.tensor_scalar(
                        Q_vg[1][:, 0:512], src,
                        thr1n, 2.0, ALU.is_le, ALU.mult), "vm")
                    v = P.inc("ve", lambda e, dstB=dstB: e.tensor_tensor(
                        dstB, Q_vg[0][:, 0:512], Q_vg[1][:, 0:512],
                        ALU.subtract), "vm")
                    sm_w1sg[i] = ("vm", v)
                    vm_w1add[i] = v
                if ct == CT - 1:
                    vm_w1q[qgrp] = vm_w1add[i]

            for ct in range(4):
                w1q_issue(0, ct)

            P.wait("sy", "tm", tm_p3[NT - 1])
            for tt in range(NT):
                P.waitv("sy", sv_x2[tt])
            P.wait("sc", "tm", tm_p3[NT - 1])
            P.waitv("sc", sv_x2[NT - 2])
            P.waitv("sc", sv_x2[NT - 1])
            sv_x2T = []
            def x2T_load(ct):
                sv_x2T.append(P.dma("sy", lambda e, ct=ct: e.dma_start(
                    A_x2T[:, ct * TC:(ct + 1) * TC],
                    x2s[:, ct * 128:(ct + 1) * 128], transpose=True), f"x2T{ct % 4}"))
            for ct in range(4):
                x2T_load(ct)

            P.wait("ve", *sm_p2chunk[62])
            P.wait("ve", *sm_p2chunk[63])
            vm_red2 = P.inc("ve", lambda e: e.tensor_reduce(
                c_red, wsum[:, :], AX, ALU.add), "vm")
            P.wait("te", "vm", vm_red2)
            P.wait("te", "vm", vm_p3pre[NT - 1])
            tm_t2a = P.inc("te", lambda e: e.matmul(
                psB[0:1, 0:1], c_red, ones_col_f, start=True, stop=True), "tm")
            P.wait("ve", "tm", tm_t2a)
            vm_t2b = P.inc("ve", lambda e: e.tensor_scalar(
                thr_tmp[0:1, 0:1], psB[0:1, 0:1], 0.7 / (FF * C), None,
                ALU.mult), "vm")
            P.wait("te", "vm", vm_t2b)
            tm_t2b = P.inc("te", lambda e: e.matmul(
                psB[:, 512:513], ones_row_f, thr_tmp[0:1, 0:1],
                start=True, stop=True), "tm")
            P.wait("ve", "tm", tm_t2b)
            P.inc("ve", lambda e: e.tensor_copy(thr2p, psB[:, 512:513]), "vm")
            vm_thr2 = P.inc("ve", lambda e: e.tensor_scalar(
                thr2n, psB[:, 512:513], -1.0, None, ALU.mult), "vm")

            def w2_quant_ft(ft):
                for dh2 in range(2):
                    j = ft * 2 + dh2
                    par = j % 2
                    if j >= 2:
                        P.wait("sy", "sm", sm_w2sg[j - 2])
                    sv_here = P.dma("sy", lambda e, ft=ft, dh2=dh2: e.dma_start(
                        Q_w2stg[par],
                        w2T[ft * 128:(ft + 1) * 128,
                            dh2 * 1024:(dh2 + 1) * 1024]),
                        "p3a" if par == 0 else "p3b")
                    P.waitv("sc", sv_here)
                    if j == 0:
                        P.wait("sc", "vm", vm_thr2)
                    if j >= 1:
                        P.wait("sc", "vm", vm_w2add[j - 1])
                    P.inc("sc", lambda e: e.activation(
                        Q_sgn2A, Q_w2stg[par], AF.Sign, bias=thr2n), "sm")
                    sm_w2sg[j] = P.inc("sc", lambda e: e.activation(
                        Q_sgn2B, Q_w2stg[par], AF.Sign, bias=thr2p), "sm")
                    P.wait("ve", "sm", sm_w2sg[j])
                    if j >= 2:
                        P.waitv("ve", sv_w2q[j - 2])
                    vm_w2add[j] = P.inc("ve", lambda e, par=par: e.tensor_tensor(
                        Q_w2out[par], Q_sgn2A, Q_sgn2B, ALU.add), "vm")
                    P.wait("gp", "vm", vm_w2add[j])
                    sv_w2q[j] = P.dma("gp", lambda e, ft=ft, dh2=dh2, par=par:
                        e.dma_start(
                            w2qs[ft * 128:(ft + 1) * 128,
                                 dh2 * 1024:(dh2 + 1) * 1024], Q_w2out[par]),
                        f"wqs{par}")

            for fb in range(NFB):
                fgrp, fi = divmod(fb, 4)
                par = fb % 2
                ps = psA if par == 0 else psB
                if fb == 0:
                    for ct in range(CT):
                        w1q_consume(0, ct)
                        if ct + 4 < CT:
                            w1q_issue(0, ct + 4)
                        if ct >= 4:
                            x2T_load(ct)
                    for ct in range(CT):
                        w1q_issue(1, ct)
                        w1q_consume(1, ct)
                elif fb >= 4:
                    g_next = fb // 4 + 1
                    if g_next <= 15:
                        fi_ = fb % 4
                        for j in range(4):
                            w1q_issue(g_next, 4 * fi_ + j)
                            w1q_consume(g_next, 4 * fi_ + j)
                if 2 <= fb:
                    w2_quant_ft(fb - 2)
                if fb == NFB - 1:
                    w2_quant_ft(62)
                    w2_quant_ft(63)

                if fb != 0:
                    P.wait("te", "vm", vm_w1q[fgrp])
                if fb == 1:
                    P.wait("te", "vm", vm_thr2)
                if fb >= 2:
                    P.wait("te", "sm", sm_gelu[fb - 2])
                if fb == 0:
                    P.wait("te", "vm", vm_p3pre[NT - 2])
                    for ct in range(CT):
                        P.waitv("te", sv_x2T[ct])
                        P.wait("te", "vm", vm_w1add[ct])
                        def _mm1c(t, ct=ct, fgrp=fgrp, fi=fi, ps=ps):
                            lhs = B_w1q[fgrp % 4][:, ct * 512 + fi * 128:
                                                  ct * 512 + fi * 128 + 128]
                            for ts in range(4):
                                mm = t.matmul(
                                    ps[:, ts * 512:(ts + 1) * 512], lhs,
                                    A_x2T[:, ct * TC + ts * 512:
                                          ct * TC + (ts + 1) * 512],
                                    start=(ct == 0), stop=(ct == CT - 1))
                            return mm
                        tm_mm1[fb] = P.inc("te", _mm1c, "tm")
                else:
                    def _mm1(t, fgrp=fgrp, fi=fi, ps=ps):
                        for ct in range(CT):
                            lhs = B_w1q[fgrp % 4][:, ct * 512 + fi * 128:
                                                  ct * 512 + fi * 128 + 128]
                            for ts in range(4):
                                mm = t.matmul(
                                    ps[:, ts * 512:(ts + 1) * 512], lhs,
                                    A_x2T[:, ct * TC + ts * 512:
                                          ct * TC + (ts + 1) * 512],
                                    start=(ct == 0), stop=(ct == CT - 1))
                        return mm
                    tm_mm1[fb] = P.inc("te", _mm1, "tm")

                P.wait("sc", "tm", tm_mm1[fb])
                if fb == 0:
                    P.waitv("sc", sv_b1)
                if fb >= 2:
                    P.waitv("sc", sv_ht[fb - 2])
                sm_gelu[fb] = P.inc("sc", lambda e, fb=fb, par=par: e.activation(
                    S_hstage[par], (psA if par == 0 else psB)[:, :], AF.Gelu,
                    bias=c_b1[:, fb:fb + 1], scale=0.5), "sm")
                P.wait("gp", "sm", sm_gelu[fb])
                sv_ht[fb] = P.dma("gp", lambda e, fb=fb, par=par: e.dma_start(
                    hts[fb * 128:(fb + 1) * 128, :], S_hstage[par]), f"ht{fb % 2}")

            tmA_m2, tmB_m2 = {}, {}
            vm_m2pre = {}
            sm_m2done = {}
            sv_pre0 = {}
            slot_tm = [0, 0]
            so_hist = []
            vm_b2b_prev = [0]

            def load_w2half(half):
                svs = []
                for blk in range(8):
                    ftb = blk * 8
                    dst = B_w2lo if blk < 4 else A_w2hi
                    off_el = (ftb % 32) * 1024
                    def _ld(e, ftb=ftb, dst=dst, off_el=off_el, half=half):
                        d = dst[:, off_el: off_el + 8 * 1024].rearrange(
                            "p (f d) -> p f d", f=8)
                        s = w2qs[ftb * 128:(ftb + 8) * 128,
                                 half * 1024:(half + 1) * 1024].rearrange(
                            "(f p) d -> p f d", p=128)
                        return e.dma_start(d, s)
                    svs.append(P.dma("sy", _ld, f"wh{blk % 2}"))
                return svs

            P.wait("sy", "tm", tm_mm1[NFB - 1])
            P.waitv("sy", sv_w2q[126])
            P.waitv("sy", sv_w2q[127])
            for half in range(2):
                if half == 1:
                    P.wait("sy", "tm", tmB_m2[(0, NT - 1)])
                sv_w2h = load_w2half(half)
                sv_b2 = P.dma("sy", lambda e, half=half: e.dma_start(
                    Q_b2row, b2r_in[0:1, half * 1024:(half + 1) * 1024]), "mi")
                P.waitv("te", sv_b2)
                if half == 0:
                    P.wait("te", "sm", sm_gelu[NFB - 1])
                else:
                    P.wait("te", "vm", vm_b2b_prev[0])
                def _b2mm(t):
                    for i in range(2):
                        mm = t.matmul(psB[:, i * 512:(i + 1) * 512],
                                      ones_row_f, Q_b2row[0:1, i * 512:(i + 1) * 512],
                                      start=True, stop=True)
                    return mm
                tm_b2 = P.inc("te", _b2mm, "tm")
                P.wait("ve", "tm", tm_b2)
                vm_b2b = P.inc("ve", lambda e: e.tensor_copy(Q_b2b, psB[:, 0:1024]), "vm")
                vm_b2b_prev[0] = vm_b2b

                def ht_load(e, pair, slot):
                    d = S_ht2[slot].rearrange("p (f t) -> p f t", f=32)
                    fb0 = slot * 32
                    s = hts[fb0 * 128:(fb0 + 32) * 128,
                            pair * 256:(pair + 1) * 256].rearrange(
                        "(f p) t -> p f t", p=128)
                    return e.dma_start(d, s)
                if half == 0:
                    P.waitv("gp", sv_ht[NFB - 2])
                    P.waitv("gp", sv_ht[NFB - 1])
                P.wait("gp", "tm", slot_tm[0])
                cd_next = [P.dma("gp", lambda e: ht_load(e, 0, 0), "hs0"), None]
                P.wait("gp", "tm", slot_tm[1])
                cd_next[1] = P.dma("gp", lambda e: ht_load(e, 0, 1), "hs1")

                for pair in range(NT // 2):
                    lo, hi = 2 * pair, 2 * pair + 1
                    acc_lo = psA[:, 0:1024]
                    acc_hi = psA[:, 1024:2048]
                    cd0, cd1 = cd_next

                    vm_r2t = {}
                    for tt in (lo, hi):
                        if (half, tt - 2) in vm_m2pre:
                            P.wait("sy", "vm", vm_m2pre[(half, tt - 2)])
                        elif half == 1 and tt < 2:
                            P.wait("sy", "vm", vm_m2pre[(0, NT - 2 + tt)])
                        sv_x2r = P.dma("sy", lambda e, tt=tt, half=half: e.dma_start(
                            T_x2res[tt % 2],
                            x2s[tt * 128:(tt + 1) * 128,
                                half * 1024:(half + 1) * 1024]), f"x2r{tt % 2}")
                        P.waitv("ve", sv_x2r)
                        P.wait("ve", "vm", vm_b2b)
                        vm_r2t[tt] = P.inc("ve", lambda e, tt=tt: e.tensor_tensor(
                            Q_r2[tt % 2], T_x2res[tt % 2], Q_b2b, ALU.add), "vm")

                    sv_p0rd = {}
                    if half == 1:
                        for tt in (lo, hi):
                            if (1, tt - 2) in sm_m2done:
                                P.wait("sy", "sm", sm_m2done[(1, tt - 2)])
                            P.waitv("sy", sv_pre0[tt])
                            sv_p0rd[tt] = P.dma("sy", lambda e, tt=tt: e.dma_start(
                                Q_p0rd[tt % 2], pre0s[tt * 128:(tt + 1) * 128, :]),
                                f"p0r{tt % 2}")

                    if half == 0 and pair == 0:
                        P.wait("te", "sm", sm_gelu[NFB - 2])
                        for sv in sv_w2h:
                            P.waitv("te", sv)
                    if half == 1 and pair == 0:
                        for sv in sv_w2h:
                            P.waitv("te", sv)
                    P.waitv("te", cd0)
                    def _mpart(t, acc, slot, toff, rhs_t, start, stop):
                        for ft in range(32):
                            lhs = S_ht2[slot][:, ft * 256 + toff: ft * 256 + toff + 128]
                            for i in range(2):
                                mm = t.matmul(
                                    acc[:, i * 512:(i + 1) * 512], lhs,
                                    rhs_t[:, ft * 1024 + i * 512:
                                          ft * 1024 + (i + 1) * 512],
                                    start=(start and ft == 0),
                                    stop=(stop and ft == 31))
                        return mm
                    if (half, lo - 2) in vm_m2pre:
                        P.wait("te", "vm", vm_m2pre[(half, lo - 2)])
                    elif half == 1 and lo < 2:
                        P.wait("te", "vm", vm_m2pre[(0, NT - 2 + lo)])
                    P.inc("te", lambda e: _mpart(e, acc_lo, 0, 0, B_w2lo, True, False),
                          "tm")
                    if (half, hi - 2) in vm_m2pre:
                        P.wait("te", "vm", vm_m2pre[(half, hi - 2)])
                    elif half == 1 and hi < 2:
                        P.wait("te", "vm", vm_m2pre[(0, NT - 2 + hi)])
                    tmA = P.inc("te", lambda e: _mpart(e, acc_hi, 0, 128, B_w2lo,
                                                       True, False), "tm")
                    slot_tm[0] = tmA
                    P.waitv("te", cd1)
                    tm_lo = P.inc("te", lambda e: _mpart(e, acc_lo, 1, 0, A_w2hi,
                                                         False, True), "tm")
                    tmA_m2[(half, lo)] = tm_lo
                    tm_hi = P.inc("te", lambda e: _mpart(e, acc_hi, 1, 128, A_w2hi,
                                                         False, True), "tm")
                    tmA_m2[(half, hi)] = tm_hi
                    tmB_m2[(half, lo)] = tm_lo
                    tmB_m2[(half, hi)] = tm_hi
                    slot_tm[1] = tm_hi

                    if pair + 1 < NT // 2:
                        P.wait("gp", "tm", slot_tm[0])
                        cd_next = [P.dma("gp", lambda e, pair=pair: ht_load(
                            e, pair + 1, 0), "hs0"), None]
                        P.wait("gp", "tm", slot_tm[1])
                        cd_next[1] = P.dma("gp", lambda e, pair=pair: ht_load(
                            e, pair + 1, 1), "hs1")

                    for tt in (lo, hi):
                        acc = acc_lo if tt == lo else acc_hi
                        P.wait("ve", "tm", tmB_m2[(half, tt)])
                        P.wait("ve", "vm", vm_r2t[tt])
                        if (half, tt - 2) in sm_m2done:
                            P.wait("ve", "sm", sm_m2done[(half, tt - 2)])
                            if half == 0:
                                P.waitv("ve", sv_pre0[tt - 2])
                        elif half == 1 and tt < 2:
                            P.wait("ve", "sm", sm_m2done[(0, NT - 2 + tt)])
                            P.waitv("ve", sv_pre0[NT - 2 + tt])
                        stat_s1 = s1a[:, tt:tt + 1] if half == 0 else ln_s1[tt % 2]
                        vm_pre = P.inc("ve", lambda e, acc=acc, tt=tt, stat_s1=stat_s1:
                            e.scalar_tensor_tensor(
                                Q_m2pre[tt % 2], acc, 0.5, Q_r2[tt % 2],
                                ALU.mult, ALU.add, accum_out=stat_s1), "vm")
                        vm_m2pre[(half, tt)] = vm_pre
                        stat_s2 = s2a[:, tt:tt + 1] if half == 0 else ln_s2[tt % 2]
                        P.wait("sc", "vm", vm_pre)
                        sm_sq2 = P.inc("sc", lambda e, tt=tt, stat_s2=stat_s2:
                            e.activation(T_sq[:, 0:DHALF], Q_m2pre[tt % 2],
                                         AF.Square, accum_out=stat_s2), "sm")

                        if half == 0:
                            P.wait("gp", "vm", vm_pre)
                            sv_pre0[tt] = P.dma("gp", lambda e, tt=tt: e.dma_start(
                                pre0s[tt * 128:(tt + 1) * 128, :], Q_m2pre[tt % 2]),
                                f"p0{tt % 2}")
                            sm_m2done[(half, tt)] = sm_sq2
                        else:
                            par2 = tt % 2
                            P.wait("ve", "sm", sm_sq2)
                            P.inc("ve", lambda e, tt=tt, par2=par2: e.tensor_tensor(
                                ln_s1[par2], ln_s1[par2], s1a[:, tt:tt + 1],
                                ALU.add), "vm")
                            P.inc("ve", lambda e, tt=tt, par2=par2: e.tensor_tensor(
                                ln_s2[par2], ln_s2[par2], s2a[:, tt:tt + 1],
                                ALU.add), "vm")
                            P.inc("ve", lambda e, par2=par2: e.tensor_scalar(
                                ln_mean[par2], ln_s1[par2], 1.0 / C, None,
                                ALU.mult), "vm")
                            P.inc("ve", lambda e, par2=par2: e.tensor_scalar(
                                ln_var[par2], ln_s2[par2], 1.0 / C, None,
                                ALU.mult), "vm")
                            P.inc("ve", lambda e, par2=par2: e.scalar_tensor_tensor(
                                ln_nm[par2], ln_mean[par2], -1.0, ln_mean[par2],
                                ALU.mult, ALU.mult), "vm")
                            vm_v = P.inc("ve", lambda e, par2=par2: e.tensor_tensor(
                                ln_var[par2], ln_var[par2], ln_nm[par2],
                                ALU.add), "vm")
                            P.wait("sc", "vm", vm_v)
                            sm_sd2 = P.inc("sc", lambda e, par2=par2: e.activation(
                                ln_sd[par2], ln_var[par2], AF.Sqrt,
                                bias=eps_ln), "sm")
                            P.wait("ve", "sm", sm_sd2)
                            P.inc("ve", lambda e, par2=par2: e.reciprocal(
                                ln_rstd[par2], ln_sd[par2]), "vm")
                            vm_cf = P.inc("ve", lambda e, par2=par2:
                                e.scalar_tensor_tensor(
                                    ln_nm[par2], ln_mean[par2], -1.0, ln_rstd[par2],
                                    ALU.mult, ALU.mult), "vm")
                            P.wait("sc", "vm", vm_cf)
                            P.waitv("sc", sv_p0rd[tt])
                            sm_last = 0
                            for piece in range(2):
                                src = Q_p0rd[tt % 2] if piece == 0 else Q_m2pre[tt % 2]
                                for seg in range(2):
                                    opar = (piece * 2 + seg) % 2
                                    if len(so_hist) >= 2:
                                        P.waitv("sc", so_hist[-2])
                                    sm_ap = P.inc("sc", lambda e, src=src, seg=seg,
                                                  opar=opar, par2=par2: e.activation(
                                        T_ost[opar], src[:, seg * 512:(seg + 1) * 512],
                                        AF.Identity, bias=ln_nm[par2],
                                        scale=ln_rstd[par2]), "sm")
                                    sm_last = sm_ap
                                    P.wait("sy", "sm", sm_ap)
                                    col0 = piece * 1024 + seg * 512
                                    soh = P.dma("sy", lambda e, tt=tt, col0=col0,
                                                opar=opar: e.dma_start(
                                        out_t[tt * 128:(tt + 1) * 128,
                                              col0:col0 + 512],
                                        T_ost[opar]), f"so{opar}")
                                    so_hist.append(soh)
                            sm_m2done[(half, tt)] = sm_last

            P.wait("sy", "so0", P.c["so0"])
            P.wait("sy", "so1", P.c["so1"])

        def mk(eng_name):
            def body(eng):
                prog(Prog(eng_name, eng, sems))
            return body
        block.sync(mk("sy"))
        block.gpsimd(mk("gp"))
        block.vector(mk("ve"))
        block.scalar(mk("sc"))
        block.tensor(mk("te"))

    return nc


_NC_CACHE = []


def kernel(**inputs):
    _ = inputs.get("L0_attn_cos"), inputs.get("L0_attn_sin")
    x = np.asarray(inputs["L0_attn_x"], dtype=np.float32).reshape(T, C)
    wqT = np.ascontiguousarray(np.asarray(inputs["L0_attn_Wq"], np.float32).T)
    wkT = np.ascontiguousarray(np.asarray(inputs["L0_attn_Wk"], np.float32).T)
    wvT = np.ascontiguousarray(np.asarray(inputs["L0_attn_Wv"], np.float32).T)
    woT = np.ascontiguousarray(np.asarray(inputs["L0_attn_Wo"], np.float32).T)
    w1T = np.ascontiguousarray(np.asarray(inputs["L0_ffn_w1"], np.float32).T)
    w2T = np.ascontiguousarray(np.asarray(inputs["L0_ffn_w2"], np.float32).T)
    b1c = np.ascontiguousarray(
        np.asarray(inputs["L0_ffn_b1"], np.float32).reshape(NFB, 128).T)
    b2r = np.ascontiguousarray(
        np.asarray(inputs["L0_ffn_b2"], np.float32).reshape(1, C))

    nc = build_nc()
    in_maps = []
    for c in range(NCORES):
        xs = np.ascontiguousarray(x[c * TC:(c + 1) * TC, :])
        in_maps.append(dict(
            x=xs, xT=np.ascontiguousarray(xs.T),
            wqT=wqT, wkT=wkT, wvT=wvT, woT=woT,
            w1T=w1T, w2T=w2T, b1c=b1c, b2r=b2r))
    res = run_bass_kernel_spmd(nc, in_maps, core_ids=list(range(NCORES)))
    out = np.concatenate([res.results[c]["out"] for c in range(NCORES)], axis=0)
    return out.reshape(1, T, C)


if __name__ == "__main__":
    build_nc()
    print("build ok")


# version 18
# speedup vs baseline: 1.0892x; 1.0152x over previous
import contextlib
import numpy as np
import concourse.bass as bass
import concourse.mybir as mybir
from concourse.bass_utils import run_bass_kernel_spmd

F32 = mybir.dt.float32
BF16 = mybir.dt.bfloat16
U8 = mybir.dt.uint8
AF = mybir.ActivationFunctionType
ALU = mybir.AluOpType
AX = mybir.AxisListType.X

NCORES = 8
T = 16384
C = 2048
TC = T // NCORES
NT = TC // 128
CT = C // 128
H = 16
FF = 8192
NFB = FF // 128
DHALF = 1024

KSLOT = 129
KV_COLS = H * KSLOT
VSLOT = 144

A_OFF = 0
B_OFF = 65536
S_OFF = 131072
Q_OFF = 163840
T_OFF = 196608
C_OFF = 209600
ARENA_BYTES = 212800


class Prog:
    def __init__(self, which, handle, sems):
        self.which = which
        self.handle = handle
        self.sems = sems
        self.c = {k: 0 for k in sems}

    def do(self, eng, fn):
        if self.which == eng:
            return fn(self.handle)

    CHAIN = {"ve": "vm", "sc": "sm"}

    def inc(self, eng, fn, sem, n=1):
        ch = self.CHAIN.get(eng)
        if ch is not None and self.which == eng:
            self.handle.wait_ge(self.sems[ch], self.c[ch])
        self.c[sem] += n
        if self.which == eng:
            fn(self.handle).then_inc(self.sems[sem], n)
        return self.c[sem]

    def dma(self, eng, fn, stream):
        tot = self.c[stream]
        if self.which == eng and tot > 0:
            self.handle.wait_ge(self.sems[stream], tot)
        self.c[stream] += 16
        if self.which == eng:
            fn(self.handle).then_inc(self.sems[stream], 16)
        return (stream, self.c[stream])

    def waitv(self, eng, sv):
        self.wait(eng, sv[0], sv[1])

    def inc_default(self, eng, fn, sem):
        self.c[sem] += 1
        if self.which == eng:
            fn(self.handle).then_inc(self.sems[sem])
        return self.c[sem]

    def wait(self, eng, sem, val):
        assert val <= self.c[sem], (eng, sem, val, self.c[sem])
        if self.which == eng:
            self.handle.wait_ge(self.sems[sem], val)


def build_nc():
    nc = bass.Bass("TRN2", target_bir_lowering=False, debug=False)

    x_in = nc.dram_tensor("x", [TC, C], F32, kind="ExternalInput").ap()
    xT_in = nc.dram_tensor("xT", [C, TC], F32, kind="ExternalInput").ap()
    wqT = nc.dram_tensor("wqT", [C, C], F32, kind="ExternalInput").ap()
    wkT = nc.dram_tensor("wkT", [C, C], F32, kind="ExternalInput").ap()
    wvT = nc.dram_tensor("wvT", [C, C], F32, kind="ExternalInput").ap()
    woT = nc.dram_tensor("woT", [C, C], F32, kind="ExternalInput").ap()
    w1T = nc.dram_tensor("w1T", [C, FF], F32, kind="ExternalInput").ap()
    w2T = nc.dram_tensor("w2T", [FF, C], F32, kind="ExternalInput").ap()
    b1c_in = nc.dram_tensor("b1c", [128, NFB], F32, kind="ExternalInput").ap()
    b2r_in = nc.dram_tensor("b2r", [1, C], F32, kind="ExternalInput").ap()
    out_t = nc.dram_tensor("out", [TC, C], F32, kind="ExternalOutput").ap()

    x2s = nc.dram_tensor("x2s", [TC, C], BF16).ap()
    hts = nc.dram_tensor("hts", [FF, TC], BF16).ap()
    w2qs = nc.dram_tensor("w2qs", [FF, C], BF16).ap()
    pre0s = nc.dram_tensor("pre0s", [TC, DHALF], F32).ap()
    cc_in1 = nc.dram_tensor("cc_in1", [128, KV_COLS // 2], F32)
    cc_out1 = nc.dram_tensor("cc_out1", [128, KV_COLS // 2], F32, addr_space="Shared")
    cc_in2 = nc.dram_tensor("cc_in2", [128, KV_COLS // 2], F32)
    cc_out2 = nc.dram_tensor("cc_out2", [128, KV_COLS // 2], F32, addr_space="Shared")

    with contextlib.ExitStack() as es:
        E = es.enter_context
        arena = E(nc.sbuf_tensor("arena", [128, ARENA_BYTES], U8))
        ar = arena.ap()

        def V(off, cols, dt, rows=128, tag=""):
            nbytes = cols * (4 if dt == F32 else 2)
            assert off % 32 == 0 and off + nbytes <= ARENA_BYTES, (tag, off, nbytes)
            return ar[0:rows, off:off + nbytes].bitcast(dt)

        co = [C_OFF]
        def CV(cols, dt, rows=128):
            v = V(co[0], cols, dt, rows)
            co[0] += ((cols * (4 if dt == F32 else 2) + 63) // 64) * 64
            return v
        ones_row_f = CV(128, F32, rows=1)
        ones_row_b = CV(128, BF16, rows=1)
        ones_col_f = CV(1, F32)
        eps_ln = CV(1, F32)
        thr1p, thr1n = CV(1, F32), CV(1, F32)
        thr2p, thr2n = CV(1, F32), CV(1, F32)
        c_b1 = CV(NFB, F32)
        wsum = CV(64, F32)
        c_red = CV(1, F32)
        thr_tmp = CV(4, F32, rows=1)
        eps_z6 = CV(1, F32, rows=1)
        ln_s1 = [CV(1, F32), CV(1, F32)]
        ln_s2 = [CV(1, F32), CV(1, F32)]
        ln_mean = [CV(1, F32), CV(1, F32)]
        ln_var = [CV(1, F32), CV(1, F32)]
        ln_sd = [CV(1, F32), CV(1, F32)]
        ln_rstd = [CV(1, F32), CV(1, F32)]
        ln_nm = [CV(1, F32), CV(1, F32)]
        s1a = CV(NT, F32)
        s2a = CV(NT, F32)
        assert co[0] <= ARENA_BYTES, co

        A_xT = V(A_OFF, CT * TC, BF16, tag="A")
        A_woT = A_xT
        A_x2T = A_xT
        A_w2hi = A_xT
        B_wkT = V(B_OFF, CT * C, BF16, tag="B")
        B_yT = B_wkT
        B_w1q = [V(B_OFF + i * 16384, CT * 512, BF16, tag="Bw1") for i in range(4)]
        B_w2lo = B_wkT
        S_wv = [V(S_OFF + i * 16384, CT * 512, BF16, tag="Swv") for i in range(2)]
        S_wq = S_wv
        S_xres = [V(S_OFF + i * 8192, C, F32, tag="Sxr") for i in range(2)]
        S_x2bf = [V(S_OFF + 16384 + i * 4096, C, BF16, tag="Sx2") for i in range(2)]
        S_hstage = [V(S_OFF + i * 4096, C, BF16, tag="Shs") for i in range(2)]
        S_ht2 = [V(S_OFF + i * 16384, 32 * 256, BF16, tag="Sht2") for i in range(2)]
        T_x2res = [V(T_OFF + i * 2048, DHALF, BF16, tag="Tx2r") for i in range(2)]
        T_ost = [V(T_OFF + 4096 + i * 2048, 512, F32, tag="Tost") for i in range(2)]
        Q_p1stg = [V(Q_OFF + i * 4096, C, BF16, tag="Qp1") for i in range(2)]
        Q_absd = V(Q_OFF + 8192, C, BF16, tag="Qab")
        Q_fmin = [V(Q_OFF + 12288 + i * 2048, 512, F32, tag="Qfm") for i in range(2)]
        Q_fexp = [V(Q_OFF + 16384 + i * 2048, 512, F32, tag="Qfe") for i in range(2)]
        Q_qT = [V(Q_OFF + 20480 + i * 4096, TC, BF16, tag="QqT") for i in range(2)]
        Q_zb = [V(Q_OFF + 28672 + i * 2048, 512, F32, tag="Qzb") for i in range(2)]
        kv_loc = V(Q_OFF + 20480, KV_COLS, F32, tag="kvloc")
        Q_pre = [V(Q_OFF + 16384 + i * 8192, C, F32, tag="Qpre") for i in range(2)]
        Q_q2stg = [V(Q_OFF + i * 2048, 512, F32, tag="Qq2") for i in range(4)]
        Q_sgnA = V(Q_OFF + 8192, 1024, BF16, tag="QsA")
        Q_sgnB = V(Q_OFF + 10240, 1024, BF16, tag="QsB")
        Q_w2stg = [V(Q_OFF + 12288 + i * 4096, 1024, F32, tag="Qw2s") for i in range(2)]
        Q_sgn2A = V(Q_OFF + 20480, 1024, BF16, tag="Qs2A")
        Q_sgn2B = V(Q_OFF + 22528, 1024, BF16, tag="Qs2B")
        Q_w2out = [V(Q_OFF + 24576 + i * 2048, 1024, BF16, tag="Qw2o") for i in range(2)]
        Q_vg = [V(Q_OFF + 28672 + i * 2048, 1024, BF16, tag="Qvg") for i in range(2)]
        Q_m2pre = [V(Q_OFF + i * 4096, DHALF, F32, tag="Qm2p") for i in range(2)]
        Q_r2 = [V(Q_OFF + 8192 + i * 4096, DHALF, F32, tag="Qr2") for i in range(2)]
        Q_b2b = V(Q_OFF + 16384, DHALF, F32, tag="Qb2b")
        Q_p0rd = [V(Q_OFF + 20480 + i * 4096, DHALF, F32, tag="Qp0") for i in range(2)]
        Q_b2row = V(Q_OFF + 28672, DHALF, F32, rows=1, tag="Qb2r")
        T_kbf = [V(T_OFF + i * 1024, 512, BF16, tag="Tk") for i in range(2)]
        T_vaug = [V(T_OFF + 2048 + i * 1280, 4 * VSLOT, BF16, tag="Tv") for i in range(2)]
        T_kvbf = V(T_OFF + 4608, KV_COLS, BF16, tag="Tkv")
        T_sq = V(T_OFF + 8832, C, BF16, tag="Tsq")
        z_ln = V(T_OFF + 8832, 512, F32, rows=1, tag="zln")
        z_bf = [V(T_OFF + i * 1024, 512, BF16, rows=1, tag="zbf")
                for i in range(4)]
        assert T_OFF + 8832 + 2 * C <= C_OFF

        psA = E(nc.psum_tensor("psA", [128, 2048], F32))
        psB = E(nc.psum_tensor("psB", [128, 2048], F32))

        sems = {}
        stream_names = (
            ["tm", "vm", "sm", "cc", "mi"]
            + [f"xt{i}" for i in range(4)] + [f"wk{i}" for i in range(4)]
            + ["wv0", "wv1", "q1a", "q1b", "wo0", "wo1", "mig",
               "x2T0", "x2T1", "x2T2", "x2T3",
               "xr0", "xr1", "x20", "x21", "p2a", "p2b", "p3a", "p3b",
               "p4a", "p4b",
               "wqs0", "wqs1", "ht0", "ht1", "wh0", "wh1",
               "hs0", "hs1", "hs2", "x2r0", "x2r1",
               "p00", "p01", "p0r0", "p0r1", "so0", "so1"])
        for name in stream_names:
            sems[name] = E(nc.semaphore(name))
        block = E(nc.Block())

        def prog(P: Prog):
            P.inc("ve", lambda e: e.memset(ones_row_f, 1.0), "vm")
            P.inc("ve", lambda e: e.memset(ones_row_b, 1.0), "vm")
            P.inc("ve", lambda e: e.memset(ones_col_f, 1.0), "vm")
            P.inc("ve", lambda e: e.memset(eps_ln, 1e-5), "vm")
            P.inc("ve", lambda e: e.memset(eps_z6, 1e-6), "vm")
            P.inc("ve", lambda e: e.memset(T_vaug[0][:, :], 0.0), "vm")
            P.inc("ve", lambda e: e.memset(T_vaug[1][:, :], 0.0), "vm")
            for par in range(2):
                for hh in range(4):
                    P.inc("ve", lambda e, par=par, hh=hh: e.memset(
                        T_vaug[par][:, hh * VSLOT + 128: hh * VSLOT + 129], 1.0), "vm")
            vm_setup_done = P.c["vm"]

            sv_b1 = P.dma("sy", lambda e: e.dma_start(c_b1, b1c_in[:, :]), "mi")

            sv_xt = [None] * CT
            sv_wk = [None] * CT
            def _wslot_load(e, src_t, g, par):
                dst = S_wv[par].rearrange("p (ct o) -> p ct o", ct=CT)
                src = src_t[:, g * 512:(g + 1) * 512].rearrange(
                    "(ct p) o -> p ct o", p=128)
                return e.dma_start(dst, src)
            sv_wv0 = P.dma("gp", lambda e: _wslot_load(e, wvT, 0, 0), "wv0")
            for ct in range(CT):
                sv_xt[ct] = P.dma("gp", lambda e, ct=ct: e.dma_start(
                    A_xT[:, ct * TC:(ct + 1) * TC],
                    xT_in[ct * 128:(ct + 1) * 128, :]), f"xt{ct % 4}")
                sv_wk[ct] = P.dma("gp", lambda e, ct=ct: e.dma_start(
                    B_wkT[:, ct * C:(ct + 1) * C],
                    wkT[ct * 128:(ct + 1) * 128, :]), f"wk{ct % 4}")
            sv_wv = [None] * 4
            sv_wv[0] = sv_wv0
            sv_wv[1] = P.dma("gp", lambda e: _wslot_load(e, wvT, 1, 1), "wv1")

            vm_kv_evac = [0] * 4
            sm_vaug = {}
            tm_kmm = {}
            tm_kvmm = {}
            vm_fm = {}
            sm_p1chunk = {}

            def p1_pass1_chunk(i, src_t):
                rt, fc = divmod(i, 4)
                par = i % 2
                if i >= 2:
                    P.wait("gp", *sm_p1chunk[i - 2])
                sv_here = P.dma("gp", lambda e: e.dma_start(
                    Q_p1stg[par],
                    src_t[rt * 128:(rt + 1) * 128, fc * 2048:(fc + 1) * 2048]),
                    "q1a" if par == 0 else "q1b")
                if par == 0:
                    P.waitv("sc", sv_here)
                    v = P.inc("sc", lambda e, i=i: e.activation(
                        Q_absd, Q_p1stg[0], AF.Abs, accum_out=wsum[:, i:i + 1]), "sm")
                    sm_p1chunk[i] = ("sm", v)
                else:
                    P.waitv("ve", sv_here)
                    v = P.inc("ve", lambda e, i=i: e.tensor_reduce(
                        wsum[:, i:i + 1], Q_p1stg[1], AX, ALU.add,
                        apply_absolute_value=True), "vm")
                    sm_p1chunk[i] = ("vm", v)

            sv_wq0_early = [None]

            def _emit_kv_acc(g, tt):
                kpar = tt % 2
                P.wait("te", "vm", vm_fm[(g, tt)])
                P.wait("te", "sm", sm_vaug[(g, tt)])
                if tt == 0 and g > 0:
                    P.wait("te", "vm", vm_kv_evac[g - 1])
                def _kv_acc(t, tt=tt, kpar=kpar):
                    for hh in range(4):
                        mm = t.matmul(
                            psB[:, hh * 512: hh * 512 + 132],
                            T_kbf[kpar][:, hh * 128:(hh + 1) * 128],
                            T_vaug[kpar][:, hh * VSLOT: hh * VSLOT + 132],
                            start=(tt == 0), stop=(tt == NT - 1))
                    return mm
                tm_kvmm[(g, tt)] = P.inc("te", _kv_acc, "tm")

            for g in range(4):
                for tt in range(NT):
                    kpar = tt % 2
                    k_sl = psA[:, kpar * 512:(kpar + 1) * 512]
                    v_sl = psA[:, 1024 + kpar * 512: 1024 + (kpar + 1) * 512]
                    if g > 0 or tt > 0:
                        P.waitv("te", sv_wv[g])
                    prev = (g, tt - 2) if tt >= 2 else (g - 1, NT - 2 + tt)
                    if prev in vm_fm:
                        P.wait("te", "vm", vm_fm[prev])
                        P.wait("te", "sm", sm_vaug[prev])
                    if g == 0 and tt == 0:
                        P.wait("te", "vm", vm_setup_done)
                        P.waitv("te", sv_wv[0])
                        for ct in range(CT):
                            P.waitv("te", sv_xt[ct])
                            P.waitv("te", sv_wk[ct])
                            def _kv_ct(t, ct=ct):
                                lhs = A_xT[:, ct * TC: ct * TC + 128]
                                t.matmul(k_sl, lhs,
                                         B_wkT[:, ct * C: ct * C + 512],
                                         start=(ct == 0), stop=(ct == CT - 1))
                                return t.matmul(v_sl, lhs,
                                         S_wv[0][:, ct * 512:(ct + 1) * 512],
                                         start=(ct == 0), stop=(ct == CT - 1))
                            tm_here = P.inc("te", _kv_ct, "tm")
                    else:
                        def _kv_proj(t, g=g, tt=tt, k_sl=k_sl, v_sl=v_sl):
                            for ct in range(CT):
                                lhs = A_xT[:, ct * TC + tt * 128:
                                           ct * TC + tt * 128 + 128]
                                t.matmul(k_sl, lhs,
                                         B_wkT[:, ct * C + g * 512:
                                               ct * C + g * 512 + 512],
                                         start=(ct == 0), stop=(ct == CT - 1))
                                mm = t.matmul(v_sl, lhs,
                                         S_wv[g % 2][:, ct * 512:(ct + 1) * 512],
                                         start=(ct == 0), stop=(ct == CT - 1))
                            return mm
                        tm_here = P.inc("te", _kv_proj, "tm")
                    tm_kmm[(g, tt)] = tm_here

                    if tt > 0:
                        _emit_kv_acc(g, tt - 1)

                    P.wait("ve", "tm", tm_here)
                    if prev in tm_kvmm:
                        P.wait("ve", "tm", tm_kvmm[prev])
                    vmin = P.inc("ve", lambda e, k_sl=k_sl, kpar=kpar: e.tensor_scalar(
                        Q_fmin[kpar], k_sl, 0.0, None, ALU.min), "vm")
                    P.wait("sc", "vm", vmin)
                    sexp = P.inc("sc", lambda e, kpar=kpar: e.activation(
                        Q_fexp[kpar], Q_fmin[kpar], AF.Exp), "sm")
                    P.wait("ve", "sm", sexp)
                    vm_fm[(g, tt)] = P.inc("ve", lambda e, k_sl=k_sl, kpar=kpar:
                        e.scalar_tensor_tensor(
                            T_kbf[kpar][:, :], k_sl, 0.0, Q_fexp[kpar],
                            ALU.max, ALU.add), "vm")
                    P.wait("sc", "tm", tm_here)
                    if prev in tm_kvmm:
                        P.wait("sc", "tm", tm_kvmm[prev])
                    def _vcopy(e, v_sl=v_sl, kpar=kpar):
                        dst = T_vaug[kpar].rearrange(
                            "p (s c) -> p s c", s=4)[:, :, 0:128]
                        src = v_sl.rearrange("p (s c) -> p s c", s=4)
                        return e.copy(dst, src)
                    sm_vaug[(g, tt)] = P.inc("sc", _vcopy, "sm")

                    i = g * NT + tt
                    if i < 64:
                        p1_pass1_chunk(i, w1T)

                _emit_kv_acc(g, NT - 1)
                if g + 2 < 4:
                    P.wait("gp", "tm", tm_kmm[(g, NT - 1)])
                    sv_wv[g + 2] = P.dma(
                        "gp", lambda e, g=g: _wslot_load(e, wvT, g + 2, g % 2),
                        f"wv{g % 2}")
                if g == 2:
                    P.wait("gp", "tm", tm_kmm[(2, NT - 1)])
                    sv_wq0_early[0] = P.dma(
                        "gp", lambda e: _wslot_load(e, wqT, 0, 0), "wv0")
                P.wait("ve", "tm", tm_kvmm[(g, NT - 1)])
                def _kv_evac(v, g=g):
                    for hh in range(4):
                        hid = g * 4 + hh
                        r = v.tensor_copy(
                            kv_loc[:, hid * KSLOT:(hid + 1) * KSLOT],
                            psB[:, hh * 512: hh * 512 + KSLOT])
                    return r
                vm_kv_evac[g] = P.inc("ve", _kv_evac, "vm")

            P.wait("ve", *sm_p1chunk[62])
            P.wait("ve", *sm_p1chunk[63])
            vm_red1 = P.inc("ve", lambda e: e.tensor_reduce(
                c_red, wsum[:, :], AX, ALU.add), "vm")
            P.wait("te", "vm", vm_red1)
            P.wait("te", "vm", vm_kv_evac[3])
            tm_t1a = P.inc("te", lambda e: e.matmul(
                psB[0:1, 0:1], c_red, ones_col_f, start=True, stop=True), "tm")
            P.wait("ve", "tm", tm_t1a)
            vm_t1b = P.inc("ve", lambda e: e.tensor_scalar(
                thr_tmp[0:1, 0:1], psB[0:1, 0:1], 0.7 / (FF * C), None, ALU.mult), "vm")
            P.wait("te", "vm", vm_t1b)
            tm_t1b = P.inc("te", lambda e: e.matmul(
                psB[:, 512:513], ones_row_f, thr_tmp[0:1, 0:1],
                start=True, stop=True), "tm")
            P.wait("ve", "tm", tm_t1b)
            P.inc("ve", lambda e: e.tensor_copy(thr1p, psB[:, 512:513]), "vm")
            vm_thr1 = P.inc("ve", lambda e: e.tensor_scalar(
                thr1n, psB[:, 512:513], -1.0, None, ALU.mult), "vm")

            HCOL = 8 * KSLOT
            sv_wq = [None] * 4
            P.wait("gp", "vm", vm_kv_evac[1])
            sv_ccin1 = P.dma("gp", lambda e: e.dma_start(
                cc_in1[:, :], kv_loc[:, 0:HCOL]), "mig")
            P.waitv("gp", sv_ccin1)
            cc1 = P.inc_default("gp", lambda e: e.collective_compute(
                "AllReduce", ALU.add, replica_groups=[list(range(NCORES))],
                ins=[cc_in1.ap().opt()], outs=[cc_out1.ap().opt()]), "cc")
            sv_wq[0] = sv_wq0_early[0]
            P.wait("gp", "cc", cc1)
            sv_kvbf1 = P.dma("gp", lambda e: e.dma_start(
                T_kvbf[:, 0:HCOL], cc_out1[:, :]), "mig")
            P.wait("gp", "vm", vm_kv_evac[3])
            sv_ccin = P.dma("gp", lambda e: e.dma_start(
                cc_in2[:, :], kv_loc[:, HCOL:KV_COLS]), "mig")
            P.waitv("gp", sv_ccin)
            cc2 = P.inc_default("gp", lambda e: e.collective_compute(
                "AllReduce", ALU.add, replica_groups=[list(range(NCORES))],
                ins=[cc_in2.ap().opt()], outs=[cc_out2.ap().opt()]), "cc")
            P.wait("gp", "tm", tm_kmm[(3, NT - 1)])
            sv_wq[1] = P.dma("gp", lambda e: _wslot_load(e, wqT, 1, 1), "wv1")
            P.wait("gp", "cc", cc2)
            sv_kvbf = P.dma("gp", lambda e: e.dma_start(
                T_kvbf[:, HCOL:KV_COLS], cc_out2[:, :]), "mig")


            tm_q01 = [0] * H
            tm_q23 = [0] * H
            vm_fm01 = [0] * H
            vm_fm23 = [0] * H
            tm_y, tm_den, tm_zb = {}, {}, {}
            sm_zln, sm_zexp = {}, {}
            vm_zbev, vm_yev = {}, {}
            sm_p2chunk = {}
            p2i = [0]

            p2_pending = {}

            def w2_chunk_issue():
                i = p2i[0]
                if i >= 64:
                    return
                p2i[0] += 1
                par = i % 2
                if i >= 2:
                    w2_chunk_consume(i - 2)
                    P.wait("gp", *sm_p2chunk[i - 2])
                else:
                    P.wait("gp", *sm_p1chunk[62 + i])
                p2_pending[i] = P.dma("gp", lambda e, i=i: e.dma_start(
                    Q_p1stg[par], w2T[i * 128:(i + 1) * 128, :]),
                    "q1a" if par == 0 else "q1b")

            def w2_chunk_consume(i):
                if i not in p2_pending or i in sm_p2chunk:
                    return
                par = i % 2
                if par == 0:
                    P.waitv("sc", p2_pending[i])
                    v = P.inc("sc", lambda e, i=i: e.activation(
                        Q_absd, Q_p1stg[0], AF.Abs, accum_out=wsum[:, i:i + 1]), "sm")
                    sm_p2chunk[i] = ("sm", v)
                else:
                    P.waitv("ve", p2_pending[i])
                    v = P.inc("ve", lambda e, i=i: e.tensor_reduce(
                        wsum[:, i:i + 1], Q_p1stg[1], AX, ALU.add,
                        apply_absolute_value=True), "vm")
                    sm_p2chunk[i] = ("vm", v)

            def _fm_pair(h, ts0):
                qpar = h % 2
                qo = (h % 2) * 1024
                out = 0
                for ts in (ts0, ts0 + 1):
                    fpar = ts % 2
                    vmin = P.inc("ve", lambda e, ts=ts, fpar=fpar: e.tensor_scalar(
                        Q_fmin[fpar],
                        psA[:, qo + (ts % 2) * 512: qo + ((ts % 2) + 1) * 512],
                        0.0, None, ALU.min), "vm")
                    P.wait("sc", "vm", vmin)
                    sexp = P.inc("sc", lambda e, fpar=fpar: e.activation(
                        Q_fexp[fpar], Q_fmin[fpar], AF.Exp), "sm")
                    P.wait("ve", "sm", sexp)
                    out = P.inc("ve", lambda e, ts=ts, fpar=fpar, qpar=qpar:
                        e.scalar_tensor_tensor(
                            Q_qT[qpar][:, ts * 512:(ts + 1) * 512],
                            psA[:, qo + (ts % 2) * 512: qo + ((ts % 2) + 1) * 512],
                            0.0, Q_fexp[fpar], ALU.max, ALU.add), "vm")
                return out

            def _q_half(h, ts0):
                g4, hh = divmod(h, 4)
                qo = (h % 2) * 1024
                def _q(t, g4=g4, hh=hh, ts0=ts0):
                    for ct in range(CT):
                        lhs = S_wq[g4 % 2][:, ct * 512 + hh * 128:
                                           ct * 512 + hh * 128 + 128]
                        for j, ts in enumerate((ts0, ts0 + 1)):
                            mm = t.matmul(
                                psA[:, qo + j * 512: qo + (j + 1) * 512], lhs,
                                A_xT[:, ct * TC + ts * 512: ct * TC + (ts + 1) * 512],
                                start=(ct == 0), stop=(ct == CT - 1))
                    return mm
                return P.inc("te", _q, "tm")

            for step in range(H + 1):
                h = step
                hp = step - 1
                if h < H:
                    g4, hh = divmod(h, 4)
                    P.waitv("te", sv_wq[g4])
                    if h == 0:
                        P.wait("te", "vm", vm_fm[(3, NT - 1)])
                        P.wait("te", "vm", vm_thr1)
                    elif h == 1:
                        P.wait("te", "sm", sm_vaug[(3, NT - 1)])
                    else:
                        P.wait("te", "vm", vm_zbev[(h - 2, 3)])
                    tm_q01[h] = _q_half(h, 0)

                if hp >= 0:
                    qpar = hp % 2
                    if hp == 0:
                        P.waitv("te", sv_kvbf1)
                    if hp == 8:
                        P.waitv("te", sv_kvbf)
                    P.wait("te", "vm", vm_fm23[hp])
                    for ts in range(4):
                        sl = ts % 2
                        prev = (hp, ts - 2) if ts >= 2 else (hp - 1, 2 + ts)
                        if prev in vm_yev:
                            P.wait("te", "vm", vm_yev[prev])
                        tm_y[(hp, ts)] = P.inc("te", lambda e, hp=hp, ts=ts, sl=sl,
                                               qpar=qpar: e.matmul(
                            psB[:, sl * 512:(sl + 1) * 512],
                            T_kvbf[:, hp * KSLOT: hp * KSLOT + 128],
                            Q_qT[qpar][:, ts * 512:(ts + 1) * 512],
                            start=True, stop=True), "tm")
                        if prev in sm_zln:
                            P.wait("te", "sm", sm_zln[prev])
                        tm_den[(hp, ts)] = P.inc("te", lambda e, hp=hp, ts=ts, sl=sl,
                                                 qpar=qpar: e.matmul(
                            psB[0:1, 1024 + sl * 512: 1024 + (sl + 1) * 512],
                            T_kvbf[:, hp * KSLOT + 128: hp * KSLOT + 129],
                            Q_qT[qpar][:, ts * 512:(ts + 1) * 512],
                            start=True, stop=True), "tm")
                        P.wait("sc", "tm", tm_den[(hp, ts)])
                        if (hp - 1, ts) in tm_zb:
                            P.wait("sc", "tm", tm_zb[(hp - 1, ts)])
                        sm_zln[(hp, ts)] = P.inc("sc", lambda e, sl=sl: e.activation(
                            z_ln, psB[0:1, 1024 + sl * 512: 1024 + (sl + 1) * 512],
                            AF.Ln, bias=eps_z6), "sm")
                        sm_zexp[(hp, ts)] = P.inc("sc", lambda e, ts=ts: e.activation(
                            z_bf[ts], z_ln, AF.Exp, scale=-1.0), "sm")

                def _zb_part(hp, tss):
                    zo = (hp % 2) * 1024
                    for ts in tss:
                        sl = ts % 2
                        prev = (hp, ts - 2) if ts >= 2 else (hp - 1, 2 + ts)
                        if prev in vm_zbev:
                            P.wait("te", "vm", vm_zbev[prev])
                        P.wait("te", "sm", sm_zexp[(hp, ts)])
                        tm_zb[(hp, ts)] = P.inc("te", lambda e, sl=sl, ts=ts, zo=zo:
                            e.matmul(psA[:, zo + sl * 512: zo + (sl + 1) * 512],
                                     ones_row_b, z_bf[ts],
                                     start=True, stop=True), "tm")
                    for ts in tss:
                        sl = ts % 2
                        P.wait("ve", "tm", tm_zb[(hp, ts)])
                        vm_zbev[(hp, ts)] = P.inc("ve", lambda e, sl=sl, zo=zo:
                            e.tensor_copy(
                                Q_zb[sl], psA[:, zo + sl * 512: zo + (sl + 1) * 512]),
                            "vm")
                    for ts in tss:
                        sl = ts % 2
                        P.wait("ve", "tm", tm_y[(hp, ts)])
                        vm_yev[(hp, ts)] = P.inc("ve", lambda e, hp=hp, ts=ts, sl=sl:
                            e.tensor_tensor(
                                B_yT[:, hp * TC + ts * 512: hp * TC + (ts + 1) * 512],
                                psB[:, sl * 512:(sl + 1) * 512],
                                Q_zb[sl], ALU.mult), "vm")

                if hp >= 0:
                    _zb_part(hp, (0, 1))

                if h < H:
                    P.wait("ve", "tm", tm_q01[h])
                    if h == 0:
                        P.waitv("ve", sv_ccin1)
                        P.waitv("ve", sv_ccin)
                    if h >= 2:
                        P.wait("ve", "tm", tm_den[(h - 2, 3)])
                    vm_fm01[h] = _fm_pair(h, 0)

                if hp >= 0:
                    _zb_part(hp, (2, 3))

                if h < H:
                    P.wait("te", "vm", vm_fm01[h])
                    tm_q23[h] = _q_half(h, 2)
                    P.wait("ve", "tm", tm_q23[h])
                    vm_fm23[h] = _fm_pair(h, 2)

                    g4, hh = divmod(h, 4)
                    if hh == 3 and g4 + 2 <= 3:
                        P.wait("gp", "tm", tm_q23[h])
                        sv_wq[g4 + 2] = P.dma(
                            "gp", lambda e, g4=g4: _wslot_load(e, wqT, g4 + 2, g4 % 2),
                            f"wv{g4 % 2}")
                    w2_chunk_issue()
                    w2_chunk_issue()

            tm_q = tm_q23
            vm_p2 = P.c["vm"]

            P.wait("gp", "tm", tm_q[H - 1])
            sv_wo = []
            for ct in range(CT):
                sv_wo.append(P.dma("gp", lambda e, ct=ct: e.dma_start(
                    A_woT[:, ct * TC:(ct + 1) * TC],
                    woT[ct * 128:(ct + 1) * 128, :]), f"wo{ct % 2}"))

            sv_xres = {}
            P.wait("sy", "tm", tm_q[H - 1])
            for tt in range(2):
                sv_xres[tt] = P.dma("sy", lambda e, tt=tt: e.dma_start(
                    S_xres[tt % 2], x_in[tt * 128:(tt + 1) * 128, :]), f"xr{tt % 2}")
            tm_p3 = {}
            vm_p3pre = {}
            sm_x2 = {}
            sv_x2 = {}
            for tt in range(NT):
                par = tt % 2
                ps = psA if par == 0 else psB
                if tt >= 2:
                    P.wait("te", "vm", vm_p3pre[tt - 2])
                if tt == 0:
                    P.wait("te", "vm", vm_p2)
                    for ct in range(CT):
                        P.waitv("te", sv_wo[ct])
                        def _p3c(t, ct=ct, ps=ps):
                            lhs = B_yT[:, ct * TC: ct * TC + 128]
                            for ds in range(4):
                                mm = t.matmul(
                                    ps[:, ds * 512:(ds + 1) * 512], lhs,
                                    A_woT[:, ct * TC + ds * 512:
                                          ct * TC + (ds + 1) * 512],
                                    start=(ct == 0), stop=(ct == CT - 1))
                            return mm
                        tm_p3[tt] = P.inc("te", _p3c, "tm")
                else:
                    def _p3_mms(t, tt=tt, ps=ps):
                        for ct in range(CT):
                            lhs = B_yT[:, ct * TC + tt * 128: ct * TC + tt * 128 + 128]
                            for ds in range(4):
                                mm = t.matmul(
                                    ps[:, ds * 512:(ds + 1) * 512], lhs,
                                    A_woT[:, ct * TC + ds * 512:
                                          ct * TC + (ds + 1) * 512],
                                    start=(ct == 0), stop=(ct == CT - 1))
                        return mm
                    tm_p3[tt] = P.inc("te", _p3_mms, "tm")

                P.wait("ve", "tm", tm_p3[tt])
                P.waitv("ve", sv_xres[tt])
                if tt >= 2:
                    P.wait("ve", "sm", sm_x2[tt - 2])
                vm_pre = P.inc("ve", lambda e, ps=ps, par=par: e.scalar_tensor_tensor(
                    Q_pre[par], ps[:, :], 0.0, S_xres[par],
                    ALU.bypass, ALU.add, accum_out=ln_s1[par]), "vm")
                vm_p3pre[tt] = vm_pre
                if tt + 2 < NT:
                    P.wait("sy", "vm", vm_pre)
                    sv_xres[tt + 2] = P.dma("sy", lambda e, tt=tt, par=par: e.dma_start(
                        S_xres[par], x_in[(tt + 2) * 128:(tt + 3) * 128, :]),
                        f"xr{par}")
                P.wait("sc", "vm", vm_pre)
                sm_sq3 = P.inc("sc", lambda e, par=par: e.activation(
                    T_sq, Q_pre[par], AF.Square, accum_out=ln_s2[par]), "sm")
                P.wait("ve", "sm", sm_sq3)
                P.inc("ve", lambda e, par=par: e.tensor_scalar(
                    ln_mean[par], ln_s1[par], 1.0 / C, None, ALU.mult), "vm")
                P.inc("ve", lambda e, par=par: e.tensor_scalar(
                    ln_var[par], ln_s2[par], 1.0 / C, None, ALU.mult), "vm")
                P.inc("ve", lambda e, par=par: e.scalar_tensor_tensor(
                    ln_nm[par], ln_mean[par], -1.0, ln_mean[par],
                    ALU.mult, ALU.mult), "vm")
                vm_var = P.inc("ve", lambda e, par=par: e.tensor_tensor(
                    ln_var[par], ln_var[par], ln_nm[par], ALU.add), "vm")
                P.wait("sc", "vm", vm_var)
                sm_sd = P.inc("sc", lambda e, par=par: e.activation(
                    ln_sd[par], ln_var[par], AF.Sqrt, bias=eps_ln), "sm")
                P.wait("ve", "sm", sm_sd)
                P.inc("ve", lambda e, par=par: e.reciprocal(
                    ln_rstd[par], ln_sd[par]), "vm")
                vm_coef = P.inc("ve", lambda e, par=par: e.scalar_tensor_tensor(
                    ln_nm[par], ln_mean[par], -1.0, ln_rstd[par],
                    ALU.mult, ALU.mult), "vm")
                P.wait("sc", "vm", vm_coef)
                if tt >= 2:
                    P.waitv("sc", sv_x2[tt - 2])
                sm_x2[tt] = P.inc("sc", lambda e, par=par: e.activation(
                    S_x2bf[par], Q_pre[par], AF.Identity,
                    bias=ln_nm[par], scale=ln_rstd[par]), "sm")
                P.wait("gp", "sm", sm_x2[tt])
                sv_x2[tt] = P.dma("gp", lambda e, tt=tt, par=par: e.dma_start(
                    x2s[tt * 128:(tt + 1) * 128, :], S_x2bf[par]), f"x2{par}")

                w2_chunk_issue()
                w2_chunk_issue()
                w2_chunk_issue()
            while p2i[0] < 64:
                w2_chunk_issue()
                w2_chunk_issue()
            for i in range(64):
                w2_chunk_consume(i)

            sm_w1sg = {}
            vm_w1add = {}
            vm_w1q = [0] * 16
            tm_mm1 = {}
            sm_gelu = {}
            sv_ht = {}
            sm_w2sg = {}
            vm_w2add = {}
            sv_w2q = {}
            w1q_sv = {}

            def w1q_issue(qgrp, ct):
                i = qgrp * CT + ct
                b = i % 4
                if i >= 4:
                    P.wait("sy", *sm_w1sg[i - 4])
                else:
                    P.wait("sy", *sm_p2chunk[62 + i // 2])
                w1q_sv[i] = P.dma("sy", lambda e, qgrp=qgrp, ct=ct, b=b: e.dma_start(
                    Q_q2stg[b],
                    w1T[ct * 128:(ct + 1) * 128,
                        qgrp * 512:(qgrp + 1) * 512]),
                    ("p2a", "p2b", "p4a", "p4b")[b])

            def w1q_consume(qgrp, ct):
                i = qgrp * CT + ct
                b = i % 4
                par = i % 2
                src = Q_q2stg[b]
                if ct == 0:
                    if qgrp >= 4:
                        P.wait("ve", "tm", tm_mm1[(qgrp - 4) * 4 + 3])
                    else:
                        P.wait("ve", "tm", tm_p3[NT - 1])
                dstB = B_w1q[qgrp % 4][:, ct * 512:(ct + 1) * 512]
                if par == 0:
                    P.waitv("sc", w1q_sv[i])
                    if i == 0:
                        P.wait("sc", "vm", vm_thr1)
                    if i >= 2:
                        P.wait("sc", "vm", vm_w1add[i - 2])
                    P.inc("sc", lambda e, src=src: e.activation(
                        Q_sgnA[:, 0:512], src, AF.Sign, bias=thr1n), "sm")
                    s = P.inc("sc", lambda e, src=src: e.activation(
                        Q_sgnB[:, 0:512], src, AF.Sign, bias=thr1p), "sm")
                    sm_w1sg[i] = ("sm", s)
                    P.wait("ve", "sm", s)
                    vm_w1add[i] = P.inc("ve", lambda e, dstB=dstB: e.tensor_tensor(
                        dstB, Q_sgnA[:, 0:512], Q_sgnB[:, 0:512], ALU.add), "vm")
                else:
                    P.waitv("ve", w1q_sv[i])
                    if i == 1:
                        P.wait("ve", "vm", vm_thr1)
                    P.inc("ve", lambda e, src=src: e.tensor_scalar(
                        Q_vg[0][:, 0:512], src,
                        thr1p, 2.0, ALU.is_ge, ALU.mult), "vm")
                    P.inc("ve", lambda e, src=src: e.tensor_scalar(
                        Q_vg[1][:, 0:512], src,
                        thr1n, 2.0, ALU.is_le, ALU.mult), "vm")
                    v = P.inc("ve", lambda e, dstB=dstB: e.tensor_tensor(
                        dstB, Q_vg[0][:, 0:512], Q_vg[1][:, 0:512],
                        ALU.subtract), "vm")
                    sm_w1sg[i] = ("vm", v)
                    vm_w1add[i] = v
                if ct == CT - 1:
                    vm_w1q[qgrp] = vm_w1add[i]

            for ct in range(4):
                w1q_issue(0, ct)

            P.wait("sy", "tm", tm_p3[NT - 1])
            for tt in range(NT):
                P.waitv("sy", sv_x2[tt])
            P.wait("sc", "tm", tm_p3[NT - 1])
            P.waitv("sc", sv_x2[NT - 2])
            P.waitv("sc", sv_x2[NT - 1])
            sv_x2T = []
            def x2T_load(ct):
                sv_x2T.append(P.dma("sy", lambda e, ct=ct: e.dma_start(
                    A_x2T[:, ct * TC:(ct + 1) * TC],
                    x2s[:, ct * 128:(ct + 1) * 128], transpose=True), f"x2T{ct % 4}"))
            for ct in range(4):
                x2T_load(ct)

            P.wait("ve", *sm_p2chunk[62])
            P.wait("ve", *sm_p2chunk[63])
            vm_red2 = P.inc("ve", lambda e: e.tensor_reduce(
                c_red, wsum[:, :], AX, ALU.add), "vm")
            P.wait("te", "vm", vm_red2)
            P.wait("te", "vm", vm_p3pre[NT - 1])
            tm_t2a = P.inc("te", lambda e: e.matmul(
                psB[0:1, 0:1], c_red, ones_col_f, start=True, stop=True), "tm")
            P.wait("ve", "tm", tm_t2a)
            vm_t2b = P.inc("ve", lambda e: e.tensor_scalar(
                thr_tmp[0:1, 0:1], psB[0:1, 0:1], 0.7 / (FF * C), None,
                ALU.mult), "vm")
            P.wait("te", "vm", vm_t2b)
            tm_t2b = P.inc("te", lambda e: e.matmul(
                psB[:, 512:513], ones_row_f, thr_tmp[0:1, 0:1],
                start=True, stop=True), "tm")
            P.wait("ve", "tm", tm_t2b)
            P.inc("ve", lambda e: e.tensor_copy(thr2p, psB[:, 512:513]), "vm")
            vm_thr2 = P.inc("ve", lambda e: e.tensor_scalar(
                thr2n, psB[:, 512:513], -1.0, None, ALU.mult), "vm")

            def w2_quant_ft(ft):
                for dh2 in range(2):
                    j = ft * 2 + dh2
                    par = j % 2
                    if j >= 2:
                        P.wait("sy", "sm", sm_w2sg[j - 2])
                    sv_here = P.dma("sy", lambda e, ft=ft, dh2=dh2: e.dma_start(
                        Q_w2stg[par],
                        w2T[ft * 128:(ft + 1) * 128,
                            dh2 * 1024:(dh2 + 1) * 1024]),
                        "p3a" if par == 0 else "p3b")
                    P.waitv("sc", sv_here)
                    if j == 0:
                        P.wait("sc", "vm", vm_thr2)
                    if j >= 1:
                        P.wait("sc", "vm", vm_w2add[j - 1])
                    P.inc("sc", lambda e: e.activation(
                        Q_sgn2A, Q_w2stg[par], AF.Sign, bias=thr2n), "sm")
                    sm_w2sg[j] = P.inc("sc", lambda e: e.activation(
                        Q_sgn2B, Q_w2stg[par], AF.Sign, bias=thr2p), "sm")
                    P.wait("ve", "sm", sm_w2sg[j])
                    if j >= 2:
                        P.waitv("ve", sv_w2q[j - 2])
                    vm_w2add[j] = P.inc("ve", lambda e, par=par: e.tensor_tensor(
                        Q_w2out[par], Q_sgn2A, Q_sgn2B, ALU.add), "vm")
                    P.wait("gp", "vm", vm_w2add[j])
                    sv_w2q[j] = P.dma("gp", lambda e, ft=ft, dh2=dh2, par=par:
                        e.dma_start(
                            w2qs[ft * 128:(ft + 1) * 128,
                                 dh2 * 1024:(dh2 + 1) * 1024], Q_w2out[par]),
                        f"wqs{par}")

            def load_w2blk(half, blk, stream):
                ftb = blk * 8
                dst = B_w2lo if blk < 4 else A_w2hi
                off_el = (ftb % 32) * 1024
                def _ld(e, ftb=ftb, dst=dst, off_el=off_el, half=half):
                    d = dst[:, off_el: off_el + 8 * 1024].rearrange(
                        "p (f d) -> p f d", f=8)
                    s = w2qs[ftb * 128:(ftb + 8) * 128,
                             half * 1024:(half + 1) * 1024].rearrange(
                        "(f p) d -> p f d", p=128)
                    return e.dma_start(d, s)
                return P.dma("sy", _ld, stream)

            sv_w2pre = []

            for fb in range(NFB):
                fgrp, fi = divmod(fb, 4)
                par = fb % 2
                ps = psA if par == 0 else psB
                if fb == 0:
                    for ct in range(CT):
                        w1q_consume(0, ct)
                        if ct + 4 < CT:
                            w1q_issue(0, ct + 4)
                        if ct >= 4:
                            x2T_load(ct)
                    for ct in range(CT):
                        w1q_issue(1, ct)
                        w1q_consume(1, ct)
                elif fb >= 4:
                    g_next = fb // 4 + 1
                    if g_next <= 15:
                        fi_ = fb % 4
                        for j in range(4):
                            w1q_issue(g_next, 4 * fi_ + j)
                            w1q_consume(g_next, 4 * fi_ + j)
                if 2 <= fb:
                    w2_quant_ft(fb - 2)
                if fb == NFB - 1:
                    w2_quant_ft(62)
                    w2_quant_ft(63)
                if fb in (52, 56, 60):
                    b = (fb - 52) // 4
                    P.wait("sy", "tm", tm_mm1[fb - 1])
                    P.waitv("sy", sv_w2q[16 * b + 14])
                    P.waitv("sy", sv_w2q[16 * b + 15])
                    sv_w2pre.append(load_w2blk(0, b, ("wh0", "wh1", "xr0")[b]))

                if fb != 0:
                    P.wait("te", "vm", vm_w1q[fgrp])
                if fb == 1:
                    P.wait("te", "vm", vm_thr2)
                if fb >= 2:
                    P.wait("te", "sm", sm_gelu[fb - 2])
                if fb == 0:
                    P.wait("te", "vm", vm_p3pre[NT - 2])
                    for ct in range(CT):
                        P.waitv("te", sv_x2T[ct])
                        P.wait("te", "vm", vm_w1add[ct])
                        def _mm1c(t, ct=ct, fgrp=fgrp, fi=fi, ps=ps):
                            lhs = B_w1q[fgrp % 4][:, ct * 512 + fi * 128:
                                                  ct * 512 + fi * 128 + 128]
                            for ts in range(4):
                                mm = t.matmul(
                                    ps[:, ts * 512:(ts + 1) * 512], lhs,
                                    A_x2T[:, ct * TC + ts * 512:
                                          ct * TC + (ts + 1) * 512],
                                    start=(ct == 0), stop=(ct == CT - 1))
                            return mm
                        tm_mm1[fb] = P.inc("te", _mm1c, "tm")
                else:
                    def _mm1(t, fgrp=fgrp, fi=fi, ps=ps):
                        for ct in range(CT):
                            lhs = B_w1q[fgrp % 4][:, ct * 512 + fi * 128:
                                                  ct * 512 + fi * 128 + 128]
                            for ts in range(4):
                                mm = t.matmul(
                                    ps[:, ts * 512:(ts + 1) * 512], lhs,
                                    A_x2T[:, ct * TC + ts * 512:
                                          ct * TC + (ts + 1) * 512],
                                    start=(ct == 0), stop=(ct == CT - 1))
                        return mm
                    tm_mm1[fb] = P.inc("te", _mm1, "tm")

                P.wait("sc", "tm", tm_mm1[fb])
                if fb == 0:
                    P.waitv("sc", sv_b1)
                if fb >= 2:
                    P.waitv("sc", sv_ht[fb - 2])
                sm_gelu[fb] = P.inc("sc", lambda e, fb=fb, par=par: e.activation(
                    S_hstage[par], (psA if par == 0 else psB)[:, :], AF.Gelu,
                    bias=c_b1[:, fb:fb + 1], scale=0.5), "sm")
                P.wait("gp", "sm", sm_gelu[fb])
                sv_ht[fb] = P.dma("gp", lambda e, fb=fb, par=par: e.dma_start(
                    hts[fb * 128:(fb + 1) * 128, :], S_hstage[par]), f"ht{fb % 2}")

            tmA_m2, tmB_m2 = {}, {}
            vm_m2pre = {}
            sm_m2done = {}
            sv_pre0 = {}
            slot_tm = [0, 0]
            so_hist = []
            vm_b2b_prev = [0]

            P.wait("sy", "tm", tm_mm1[NFB - 1])
            P.waitv("sy", sv_w2q[126])
            P.waitv("sy", sv_w2q[127])
            for half in range(2):
                if half == 0:
                    sv_w2h = list(sv_w2pre)
                    for b, st in ((3, "xr1"), (4, "wh0"), (5, "wh1"),
                                  (6, "xr0"), (7, "x2T0")):
                        sv_w2h.append(load_w2blk(0, b, st))
                else:
                    P.wait("sy", "tm", slot_tm[0])
                    sv_w2h = [load_w2blk(1, b, ("wh0", "wh1", "xr0", "xr1")[b])
                              for b in range(4)]
                    P.wait("sy", "tm", tmB_m2[(0, NT - 1)])
                    sv_w2h += [load_w2blk(1, b, ("wh0", "wh1", "xr0", "xr1")[b - 4])
                               for b in range(4, 8)]
                sv_b2 = P.dma("sy", lambda e, half=half: e.dma_start(
                    Q_b2row, b2r_in[0:1, half * 1024:(half + 1) * 1024]), "mi")
                P.waitv("te", sv_b2)
                if half == 0:
                    P.wait("te", "sm", sm_gelu[NFB - 1])
                else:
                    P.wait("te", "vm", vm_b2b_prev[0])
                def _b2mm(t):
                    for i in range(2):
                        mm = t.matmul(psB[:, i * 512:(i + 1) * 512],
                                      ones_row_f, Q_b2row[0:1, i * 512:(i + 1) * 512],
                                      start=True, stop=True)
                    return mm
                tm_b2 = P.inc("te", _b2mm, "tm")
                P.wait("ve", "tm", tm_b2)
                vm_b2b = P.inc("ve", lambda e: e.tensor_copy(Q_b2b, psB[:, 0:1024]), "vm")
                vm_b2b_prev[0] = vm_b2b

                def ht_load(e, pair, slot):
                    d = S_ht2[slot].rearrange("p (f t) -> p f t", f=32)
                    fb0 = slot * 32
                    s = hts[fb0 * 128:(fb0 + 32) * 128,
                            pair * 256:(pair + 1) * 256].rearrange(
                        "(f p) t -> p f t", p=128)
                    return e.dma_start(d, s)
                if half == 0:
                    P.waitv("gp", sv_ht[NFB - 2])
                    P.waitv("gp", sv_ht[NFB - 1])
                P.wait("gp", "tm", slot_tm[0])
                cd_next = [P.dma("gp", lambda e: ht_load(e, 0, 0), "hs0"), None]
                P.wait("gp", "tm", slot_tm[1])
                cd_next[1] = P.dma("gp", lambda e: ht_load(e, 0, 1), "hs1")

                for pair in range(NT // 2):
                    lo, hi = 2 * pair, 2 * pair + 1
                    acc_lo = psA[:, 0:1024]
                    acc_hi = psA[:, 1024:2048]
                    cd0, cd1 = cd_next

                    vm_r2t = {}
                    for tt in (lo, hi):
                        if (half, tt - 2) in vm_m2pre:
                            P.wait("sy", "vm", vm_m2pre[(half, tt - 2)])
                        elif half == 1 and tt < 2:
                            P.wait("sy", "vm", vm_m2pre[(0, NT - 2 + tt)])
                        sv_x2r = P.dma("sy", lambda e, tt=tt, half=half: e.dma_start(
                            T_x2res[tt % 2],
                            x2s[tt * 128:(tt + 1) * 128,
                                half * 1024:(half + 1) * 1024]), f"x2r{tt % 2}")
                        P.waitv("ve", sv_x2r)
                        P.wait("ve", "vm", vm_b2b)
                        vm_r2t[tt] = P.inc("ve", lambda e, tt=tt: e.tensor_tensor(
                            Q_r2[tt % 2], T_x2res[tt % 2], Q_b2b, ALU.add), "vm")

                    sv_p0rd = {}
                    if half == 1:
                        for tt in (lo, hi):
                            if (1, tt - 2) in sm_m2done:
                                P.wait("sy", "sm", sm_m2done[(1, tt - 2)])
                            P.waitv("sy", sv_pre0[tt])
                            sv_p0rd[tt] = P.dma("sy", lambda e, tt=tt: e.dma_start(
                                Q_p0rd[tt % 2], pre0s[tt * 128:(tt + 1) * 128, :]),
                                f"p0r{tt % 2}")

                    if half == 0 and pair == 0:
                        P.wait("te", "sm", sm_gelu[NFB - 2])
                        for sv in sv_w2h:
                            P.waitv("te", sv)
                    if half == 1 and pair == 0:
                        for sv in sv_w2h:
                            P.waitv("te", sv)
                    P.waitv("te", cd0)
                    def _mpart(t, acc, slot, toff, rhs_t, start, stop):
                        for ft in range(32):
                            lhs = S_ht2[slot][:, ft * 256 + toff: ft * 256 + toff + 128]
                            for i in range(2):
                                mm = t.matmul(
                                    acc[:, i * 512:(i + 1) * 512], lhs,
                                    rhs_t[:, ft * 1024 + i * 512:
                                          ft * 1024 + (i + 1) * 512],
                                    start=(start and ft == 0),
                                    stop=(stop and ft == 31))
                        return mm
                    if (half, lo - 2) in vm_m2pre:
                        P.wait("te", "vm", vm_m2pre[(half, lo - 2)])
                    elif half == 1 and lo < 2:
                        P.wait("te", "vm", vm_m2pre[(0, NT - 2 + lo)])
                    P.inc("te", lambda e: _mpart(e, acc_lo, 0, 0, B_w2lo, True, False),
                          "tm")
                    if (half, hi - 2) in vm_m2pre:
                        P.wait("te", "vm", vm_m2pre[(half, hi - 2)])
                    elif half == 1 and hi < 2:
                        P.wait("te", "vm", vm_m2pre[(0, NT - 2 + hi)])
                    tmA = P.inc("te", lambda e: _mpart(e, acc_hi, 0, 128, B_w2lo,
                                                       True, False), "tm")
                    slot_tm[0] = tmA
                    P.waitv("te", cd1)
                    tm_lo = P.inc("te", lambda e: _mpart(e, acc_lo, 1, 0, A_w2hi,
                                                         False, True), "tm")
                    tmA_m2[(half, lo)] = tm_lo
                    tm_hi = P.inc("te", lambda e: _mpart(e, acc_hi, 1, 128, A_w2hi,
                                                         False, True), "tm")
                    tmA_m2[(half, hi)] = tm_hi
                    tmB_m2[(half, lo)] = tm_lo
                    tmB_m2[(half, hi)] = tm_hi
                    slot_tm[1] = tm_hi

                    if pair + 1 < NT // 2:
                        P.wait("gp", "tm", slot_tm[0])
                        cd_next = [P.dma("gp", lambda e, pair=pair: ht_load(
                            e, pair + 1, 0), "hs0"), None]
                        P.wait("gp", "tm", slot_tm[1])
                        cd_next[1] = P.dma("gp", lambda e, pair=pair: ht_load(
                            e, pair + 1, 1), "hs1")

                    for tt in (lo, hi):
                        acc = acc_lo if tt == lo else acc_hi
                        P.wait("ve", "tm", tmB_m2[(half, tt)])
                        P.wait("ve", "vm", vm_r2t[tt])
                        if (half, tt - 2) in sm_m2done:
                            P.wait("ve", "sm", sm_m2done[(half, tt - 2)])
                            if half == 0:
                                P.waitv("ve", sv_pre0[tt - 2])
                        elif half == 1 and tt < 2:
                            P.wait("ve", "sm", sm_m2done[(0, NT - 2 + tt)])
                            P.waitv("ve", sv_pre0[NT - 2 + tt])
                        stat_s1 = s1a[:, tt:tt + 1] if half == 0 else ln_s1[tt % 2]
                        vm_pre = P.inc("ve", lambda e, acc=acc, tt=tt, stat_s1=stat_s1:
                            e.scalar_tensor_tensor(
                                Q_m2pre[tt % 2], acc, 0.5, Q_r2[tt % 2],
                                ALU.mult, ALU.add, accum_out=stat_s1), "vm")
                        vm_m2pre[(half, tt)] = vm_pre
                        stat_s2 = s2a[:, tt:tt + 1] if half == 0 else ln_s2[tt % 2]
                        P.wait("sc", "vm", vm_pre)
                        sm_sq2 = P.inc("sc", lambda e, tt=tt, stat_s2=stat_s2:
                            e.activation(T_sq[:, 0:DHALF], Q_m2pre[tt % 2],
                                         AF.Square, accum_out=stat_s2), "sm")

                        if half == 0:
                            P.wait("gp", "vm", vm_pre)
                            sv_pre0[tt] = P.dma("gp", lambda e, tt=tt: e.dma_start(
                                pre0s[tt * 128:(tt + 1) * 128, :], Q_m2pre[tt % 2]),
                                f"p0{tt % 2}")
                            sm_m2done[(half, tt)] = sm_sq2
                        else:
                            par2 = tt % 2
                            P.wait("ve", "sm", sm_sq2)
                            P.inc("ve", lambda e, tt=tt, par2=par2: e.tensor_tensor(
                                ln_s1[par2], ln_s1[par2], s1a[:, tt:tt + 1],
                                ALU.add), "vm")
                            P.inc("ve", lambda e, tt=tt, par2=par2: e.tensor_tensor(
                                ln_s2[par2], ln_s2[par2], s2a[:, tt:tt + 1],
                                ALU.add), "vm")
                            P.inc("ve", lambda e, par2=par2: e.tensor_scalar(
                                ln_mean[par2], ln_s1[par2], 1.0 / C, None,
                                ALU.mult), "vm")
                            P.inc("ve", lambda e, par2=par2: e.tensor_scalar(
                                ln_var[par2], ln_s2[par2], 1.0 / C, None,
                                ALU.mult), "vm")
                            P.inc("ve", lambda e, par2=par2: e.scalar_tensor_tensor(
                                ln_nm[par2], ln_mean[par2], -1.0, ln_mean[par2],
                                ALU.mult, ALU.mult), "vm")
                            vm_v = P.inc("ve", lambda e, par2=par2: e.tensor_tensor(
                                ln_var[par2], ln_var[par2], ln_nm[par2],
                                ALU.add), "vm")
                            P.wait("sc", "vm", vm_v)
                            sm_sd2 = P.inc("sc", lambda e, par2=par2: e.activation(
                                ln_sd[par2], ln_var[par2], AF.Sqrt,
                                bias=eps_ln), "sm")
                            P.wait("ve", "sm", sm_sd2)
                            P.inc("ve", lambda e, par2=par2: e.reciprocal(
                                ln_rstd[par2], ln_sd[par2]), "vm")
                            vm_cf = P.inc("ve", lambda e, par2=par2:
                                e.scalar_tensor_tensor(
                                    ln_nm[par2], ln_mean[par2], -1.0, ln_rstd[par2],
                                    ALU.mult, ALU.mult), "vm")
                            P.wait("sc", "vm", vm_cf)
                            P.waitv("sc", sv_p0rd[tt])
                            sm_last = 0
                            for piece in range(2):
                                src = Q_p0rd[tt % 2] if piece == 0 else Q_m2pre[tt % 2]
                                for seg in range(2):
                                    opar = (piece * 2 + seg) % 2
                                    if len(so_hist) >= 2:
                                        P.waitv("sc", so_hist[-2])
                                    sm_ap = P.inc("sc", lambda e, src=src, seg=seg,
                                                  opar=opar, par2=par2: e.activation(
                                        T_ost[opar], src[:, seg * 512:(seg + 1) * 512],
                                        AF.Identity, bias=ln_nm[par2],
                                        scale=ln_rstd[par2]), "sm")
                                    sm_last = sm_ap
                                    P.wait("sy", "sm", sm_ap)
                                    col0 = piece * 1024 + seg * 512
                                    soh = P.dma("sy", lambda e, tt=tt, col0=col0,
                                                opar=opar: e.dma_start(
                                        out_t[tt * 128:(tt + 1) * 128,
                                              col0:col0 + 512],
                                        T_ost[opar]), f"so{opar}")
                                    so_hist.append(soh)
                            sm_m2done[(half, tt)] = sm_last

            P.wait("sy", "so0", P.c["so0"])
            P.wait("sy", "so1", P.c["so1"])

        def mk(eng_name):
            def body(eng):
                prog(Prog(eng_name, eng, sems))
            return body
        block.sync(mk("sy"))
        block.gpsimd(mk("gp"))
        block.vector(mk("ve"))
        block.scalar(mk("sc"))
        block.tensor(mk("te"))

    return nc


_NC_CACHE = []


def kernel(**inputs):
    _ = inputs.get("L0_attn_cos"), inputs.get("L0_attn_sin")
    x = np.asarray(inputs["L0_attn_x"], dtype=np.float32).reshape(T, C)
    wqT = np.ascontiguousarray(np.asarray(inputs["L0_attn_Wq"], np.float32).T)
    wkT = np.ascontiguousarray(np.asarray(inputs["L0_attn_Wk"], np.float32).T)
    wvT = np.ascontiguousarray(np.asarray(inputs["L0_attn_Wv"], np.float32).T)
    woT = np.ascontiguousarray(np.asarray(inputs["L0_attn_Wo"], np.float32).T)
    w1T = np.ascontiguousarray(np.asarray(inputs["L0_ffn_w1"], np.float32).T)
    w2T = np.ascontiguousarray(np.asarray(inputs["L0_ffn_w2"], np.float32).T)
    b1c = np.ascontiguousarray(
        np.asarray(inputs["L0_ffn_b1"], np.float32).reshape(NFB, 128).T)
    b2r = np.ascontiguousarray(
        np.asarray(inputs["L0_ffn_b2"], np.float32).reshape(1, C))

    nc = build_nc()
    in_maps = []
    for c in range(NCORES):
        xs = np.ascontiguousarray(x[c * TC:(c + 1) * TC, :])
        in_maps.append(dict(
            x=xs, xT=np.ascontiguousarray(xs.T),
            wqT=wqT, wkT=wkT, wvT=wvT, woT=woT,
            w1T=w1T, w2T=w2T, b1c=b1c, b2r=b2r))
    res = run_bass_kernel_spmd(nc, in_maps, core_ids=list(range(NCORES)))
    out = np.concatenate([res.results[c]["out"] for c in range(NCORES)], axis=0)
    return out.reshape(1, T, C)


if __name__ == "__main__":
    build_nc()
    print("build ok")
